# version 07
# baseline (speedup 1.0000x reference)
import sys

for _p in ("/opt/trn_rl_repo",):
    if _p not in sys.path:
        sys.path.insert(0, _p)

import numpy as np
import ml_dtypes
from contextlib import ExitStack

import concourse.bass as bass
import concourse.mybir as mybir
from concourse.bass_utils import run_bass_kernel_spmd

F32 = mybir.dt.float32
BF16 = mybir.dt.bfloat16
I32 = mybir.dt.int32
AF = mybir.ActivationFunctionType
ALU = mybir.AluOpType
BF = ml_dtypes.bfloat16

NCORES = 8
DM = 2048
TPC = 2048
NK = DM // 128
SEQ = 8192
INTER = 5632
NI = INTER // 128
GROUPS = [[0, 1, 2, 3], [4, 5, 6, 7]]
LN_EPS = 1e-5
LN05 = float(np.log(0.05))
QSCALE = 128.0 ** -0.5
EXP_SHIFT = -12.0

ENGINES = ("tensor", "vector", "scalar", "gpsimd", "sync")


class Prog:

    def __init__(self, nc):
        self.nc = nc
        self.q = {e: [] for e in ENGINES}
        self.semv = {}
        self.es = ExitStack()

    def sem(self, name):
        h = self.es.enter_context(self.nc.semaphore(name))
        self.semv[h] = 0
        return h

    def cur(self, s):
        return self.semv[s]

    def w(self, eng, s, val=None):
        v = self.semv[s] if val is None else val
        if v > 0:
            self.q[eng].append(lambda E, s=s, v=v: E.wait_ge(s, v))

    def do(self, eng, fn):
        self.q[eng].append(fn)

    def doi(self, eng, fn, s, amt=1):
        self.semv[s] += amt
        self.q[eng].append(lambda E, fn=fn, s=s, a=amt: fn(E).then_inc(s, a))
        return self.semv[s]

    def barrier(self, bar, engines=ENGINES):
        n = len(engines)
        base = self.semv[bar]
        for e in engines:
            self.doi(e, lambda E: E.drain(), bar, 1)
        for e in engines:
            self.w(e, bar, base + n)

    def finalize(self, block):
        for e in ENGINES:
            ops = self.q[e]
            getattr(block, e)(lambda E, ops=ops: [f(E) for f in ops])



def build(nstage=99, dbg_what=None, detect_races=False, gla_groups=16, gla_upto=7):
    nc = bass.Bass("TRN2", target_bir_lowering=False, debug=False,
                   detect_race_conditions=detect_races)
    D = nc.dram_tensor

    x_sl = D("x_sl", [DM, TPC], F32, kind="ExternalInput").ap()
    wq_sl = D("wq_sl", [DM, 512], F32, kind="ExternalInput").ap()
    wk_sl = D("wk_sl", [DM, 128], F32, kind="ExternalInput").ap()
    wv_sl = D("wv_sl", [DM, 128], F32, kind="ExternalInput").ap()
    bq_sl = D("bq_sl", [1, 512], F32, kind="ExternalInput").ap()
    bk_sl = D("bk_sl", [1, 128], F32, kind="ExternalInput").ap()
    bv_sl = D("bv_sl", [1, 128], F32, kind="ExternalInput").ap()
    gw_t = D("gw_t", [DM, INTER], F32, kind="ExternalInput").ap()
    uw_t = D("uw_t", [DM, INTER], F32, kind="ExternalInput").ap()
    dw_t = D("dw_t", [INTER, DM], F32, kind="ExternalInput").ap()
    gs_r = D("gs_r", [128, NI], F32, kind="ExternalInput").ap()
    us_r = D("us_r", [128, NI], F32, kind="ExternalInput").ap()
    ds_r = D("ds_r", [128, NK], F32, kind="ExternalInput").ap()
    wqkv_sl = D("wqkv_sl", [DM, 1536], F32, kind="ExternalInput").ap()
    wout_t = D("wout_t", [DM, DM], F32, kind="ExternalInput").ap()
    qg_t = D("qg_t", [DM, INTER], I32, kind="ExternalInput").ap()
    qu_t = D("qu_t", [DM, INTER], I32, kind="ExternalInput").ap()
    qd_t = D("qd_t", [INTER, DM], I32, kind="ExternalInput").ap()
    sg1_r = D("sg1_r", [128, NI], F32, kind="ExternalInput").ap()
    su1_r = D("su1_r", [128, NI], F32, kind="ExternalInput").ap()
    sd1_r = D("sd1_r", [128, NK], F32, kind="ExternalInput").ap()
    ident_b = D("ident_b", [128, 128], BF16, kind="ExternalInput").ap()
    mask16 = D("mask16", [128, 128], BF16, kind="ExternalInput").ap()
    mask16x = D("mask16x", [128, 128], BF16, kind="ExternalInput").ap()
    l1mask = D("l1mask", [128, 4 * 512], BF16, kind="ExternalInput").ap()
    rot_t = D("rot_t", [128, 128], BF16, kind="ExternalInput").ap()
    oidx = D("oidx", [128, NK], I32, kind="ExternalInput").ap()
    bandm = D("bandm", [128, 4], F32, kind="ExternalInput").ap()
    cos_d = D("cos_d", [128, SEQ], F32, kind="ExternalInput").ap()
    sin_d = D("sin_d", [128, SEQ], F32, kind="ExternalInput").ap()

    o_sl = D("o_sl", [DM, TPC], F32, kind="ExternalOutput").ap()
    dbg = D("dbg", [DM, TPC], F32, kind="ExternalOutput").ap() if dbg_what else None

    cc_a = D("cc_a", [DM, TPC], BF16)
    gx_a = D("gx_a", [4 * DM, TPC], BF16)
    cc_b = D("cc_b", [DM, TPC], BF16)
    gx_b = D("gx_b", [4 * DM, TPC], BF16)
    cc_c = D("cc_c", [DM, TPC], BF16)
    gx_c = D("gx_c", [4 * DM, TPC], BF16)
    cc_d = D("cc_d", [DM, TPC], BF16)
    gx_d = D("gx_d", [4 * DM, TPC], BF16)
    h_dram = D("h_dram", [INTER, TPC], BF16)
    h2_dram = D("h2_dram", [INTER, TPC], BF16)
    ffo_dram = D("ffo_dram", [DM, TPC], BF16)
    a2o_dram = D("a2o_dram", [DM, TPC], BF16)
    ffo2_dram = D("ffo2_dram", [DM, TPC], BF16)
    xspill = D("xspill", [DM, TPC], BF16)

    p = Prog(nc)
    pe = p.sem("pe")
    dv = p.sem("dv")
    sc = p.sem("sc")
    gp = p.sem("gp")
    ld = p.sem("ld")
    lw = p.sem("lw")
    st = p.sem("st")
    cc = p.sem("cc")
    bar = p.sem("bar")

    es = p.es
    sb = lambda name, shape, dt: es.enter_context(nc.sbuf_tensor(name, shape, dt))

    ones128 = sb("ones128", [128, 128], BF16)
    onesf = sb("onesf", [128, 128], F32)
    identb = sb("identb", [128, 128], BF16)
    m16 = sb("m16", [128, 128], BF16)
    onerow = sb("onerow", [1, 512], BF16)
    oidx_s = sb("oidx_s", [128, NK], I32)
    X = sb("X", [128, NK * TPC], BF16)

    Xv = [X[:, k * TPC:(k + 1) * TPC] for k in range(NK)]

    p.doi("gpsimd", lambda E: E.dma_start(
        out=X[:, :].rearrange("p (k t) -> p k t", k=NK),
        in_=x_sl.rearrange("(k p) t -> p k t", p=128)), lw, 16)
    p.do("vector", lambda E: E.memset(ones128[:, :], 1.0))
    p.do("vector", lambda E: E.memset(onesf[:, :], 1.0))
    v_dmac = p.doi("vector", lambda E: E.memset(onerow[:, :], 1.0), dv)
    p.doi("sync", lambda E: E.dma_start(out=identb[:, :], in_=ident_b), ld, 16)
    p.doi("sync", lambda E: E.dma_start(out=oidx_s[:, :], in_=oidx), ld, 16)
    p.doi("sync", lambda E: E.dma_start(out=m16[:, :], in_=mask16), ld, 16)
    p.w("tensor", dv, v_dmac)
    p.w("vector", lw, 16)

    def ln_scalars(src_views, a_t, bp_t, vtmp, vtmp2, x2buf, s1_ps, s2_ps):
        for tt in range(4):
            ts = slice(tt * 512, (tt + 1) * 512)
            if tt > 0:
                p.w("vector", pe, p.cur(pe))
            for k in range(NK - 1):
                p.do("vector", lambda E, k=k, ts=ts: E.tensor_tensor(
                    x2buf[:, k * 512:(k + 1) * 512], src_views[k][:, ts], src_views[k][:, ts], ALU.mult))
            v_x2 = p.doi("vector", lambda E, ts=ts: E.tensor_tensor(
                x2buf[:, (NK - 1) * 512:NK * 512], src_views[NK - 1][:, ts], src_views[NK - 1][:, ts], ALU.mult), dv)
            if tt > 0:
                p.w("tensor", sc, p.cur(sc))
            p.w("tensor", dv, v_x2)
            for k in range(NK):
                p.do("tensor", lambda E, k=k, ts=ts: E.matmul(
                    s1_ps[:, :], ones128[:, :], src_views[k][:, ts],
                    start=(k == 0), stop=(k == NK - 1)))
            for k in range(NK - 1):
                p.do("tensor", lambda E, k=k: E.matmul(
                    s2_ps[:, :], ones128[:, :], x2buf[:, k * 512:(k + 1) * 512],
                    start=(k == 0), stop=False))
            v_mm = p.doi("tensor", lambda E: E.matmul(
                s2_ps[:, :], ones128[:, :], x2buf[:, (NK - 1) * 512:NK * 512],
                start=False, stop=True), pe)
            p.w("scalar", pe, v_mm)
            p.do("scalar", lambda E, ts=ts: E.activation(bp_t[:, ts], s1_ps[:, :], AF.Copy, bias=0.0, scale=1.0 / DM))
            v_m = p.doi("scalar", lambda E: E.activation(vtmp[:, :], s2_ps[:, :], AF.Copy, bias=LN_EPS, scale=1.0 / DM), sc)
            p.w("vector", sc, v_m)
            p.do("vector", lambda E, ts=ts: E.tensor_tensor(vtmp2[:, :], bp_t[:, ts], bp_t[:, ts], ALU.mult))
            v_var = p.doi("vector", lambda E: E.tensor_tensor(vtmp[:, :], vtmp[:, :], vtmp2[:, :], ALU.subtract), dv)
            p.w("scalar", dv, v_var)
            p.do("scalar", lambda E: E.activation(vtmp2[:, :], vtmp[:, :], AF.Ln))
            v_a = p.doi("scalar", lambda E, ts=ts: E.activation(a_t[:, ts], vtmp2[:, :], AF.Exp, bias=0.0, scale=-0.5), sc)
            p.w("vector", sc, v_a)
            p.doi("vector", lambda E, ts=ts: E.tensor_tensor(bp_t[:, ts], bp_t[:, ts], a_t[:, ts], ALU.mult), dv)

    esP = ExitStack()
    A = _alloc(esP, nc, "bufA", [128, NK * TPC], BF16)
    Av = [A[:, k * TPC:(k + 1) * TPC] for k in range(NK)]
    a_t = _alloc(esP, nc, "a_t", [128, TPC], F32)
    bp_t = _alloc(esP, nc, "bp_t", [128, TPC], F32)
    vtmp = _alloc(esP, nc, "vtmp", [128, 512], F32)
    vtmp2 = _alloc(esP, nc, "vtmp2", [128, 512], F32)
    x2buf = _alloc(esP, nc, "x2buf", [128, NK * 512], BF16)
    esPS = ExitStack()
    s1_ps = esPS.enter_context(nc.psum_tensor("s1_ps", [128, 512], F32))
    s2_ps = esPS.enter_context(nc.psum_tensor("s2_ps", [128, 512], F32))

    if nstage >= 1:
        ln_scalars(Xv, a_t, bp_t, vtmp, vtmp2, x2buf, s1_ps, s2_ps)
        v_ln = p.cur(dv)
        p.w("vector", dv, v_ln)
        for k in range(NK):
            for tt in range(4):
                ts = slice(tt * 512, (tt + 1) * 512)
                p.do("vector", lambda E, k=k, ts=ts: E.tensor_tensor(Av[k][:, ts], Xv[k][:, ts], a_t[:, ts], ALU.mult))
                if (k, tt) != (NK - 1, 3):
                    p.do("vector", lambda E, k=k, ts=ts: E.tensor_tensor(Av[k][:, ts], Av[k][:, ts], bp_t[:, ts], ALU.subtract))
        _lts = slice(3 * 512, 4 * 512)
        v_xn = p.doi("vector", lambda E: E.tensor_tensor(
            Av[NK - 1][:, _lts], Av[NK - 1][:, _lts], bp_t[:, _lts], ALU.subtract), dv)
        if nstage >= 2:
            p.w("sync", dv, v_xn)
            p.doi("sync", lambda E: E.dma_start(
                out=cc_a.ap().rearrange("(k p) t -> p k t", p=128),
                in_=A[:, :].rearrange("p (k t) -> p k t", k=NK)), ld, 16)
            p.w("gpsimd", ld, p.cur(ld))
            p.doi("gpsimd", lambda E: E.collective_compute(
                "AllGather", ALU.bypass, replica_groups=GROUPS,
                ins=[cc_a.ap()], outs=[gx_a.ap()]), cc, 1)
    esP_open = True
    if dbg_what != "xn":
        esPS.close()
        esP.close()
        esP_open = False
    v_cca_ld = p.cur(ld)
    p.barrier(bar)
    p.w("vector", ld, v_cca_ld)
    p.w("gpsimd", ld, v_cca_ld)
    p.w("sync", ld, v_cca_ld)
    p.w("scalar", ld, v_cca_ld)
    p.w("tensor", ld, v_cca_ld)

    if nstage >= 3:
        esG = ExitStack()
        al = lambda n, s, d: esG.enter_context(nc.sbuf_tensor(n, s, d))
        wq_s = al("wq_s", [128, NK * 512], BF16)
        wk_s = al("wk_s", [128, NK * 128], BF16)
        wv_s = al("wv_s", [128, NK * 128], BF16)
        bqr = al("bqr", [1, 512], BF16)
        bkr = al("bkr", [1, 128], BF16)
        bvr = al("bvr", [1, 128], BF16)
        mx16 = al("mx16", [128, 128], BF16)
        bmask = al("bmask", [128, 4], F32)
        zcol = al("zcol", [128, 1], F32)
        xnt = al("xnt", [128, 2 * NK * 512], BF16)
        qr = al("qr", [128, 2 * 4 * 512], BF16)
        vts = al("vts", [128, 2 * 512], BF16)
        ostg = al("ostg", [128, 2 * 4 * 512], BF16)
        Ssb = al("Ssb", [128, 2 * 128], F32)
        Rf = al("Rf", [128, 3 * 128], F32)
        FP = al("FP", [128, 2 * 24 * 128], F32)
        BP = al("BP", [128, 2 * 40 * 128], BF16)
        dlp = al("dlp", [128, 2 * 8], F32)

        def fpt(pb, i, w=1):
            return FP[:, (pb * 24 + i) * 128:(pb * 24 + i + w) * 128]

        def bpt(pb, i, w=1):
            return BP[:, (pb * 40 + i) * 128:(pb * 40 + i + w) * 128]

        esGP = ExitStack()
        psA = lambda n, s, d=F32: esGP.enter_context(nc.psum_tensor(n, s, d))
        qps = psA("qps", [128, 512])
        kf_ps = psA("kf_ps", [128, 512])
        vt_ps = psA("vt_ps", [128, 512])
        w_ps = psA("w_ps", [128, 512])
        p1_ps = psA("p1_ps", [128, 512])
        o_ps = psA("o_ps", [128, 512])
        o2_ps = psA("o2_ps", [128, 512])
        tp_ps = psA("tp_ps", [128, 128], BF16)
        tp_view = tp_ps[:, :]

        def MM(out, l, r, start, stop, tp=None):
            if tp is None:
                return lambda E, o=out, l=l, r=r, s=start, t=stop: E.matmul(o, l, r, start=s, stop=t)
            return lambda E, o=out, l=l, r=r, s=start, t=stop, q=tp: E.matmul(o, l, r, start=s, stop=t, tile_position=q)

        def TT(out, a, b, op):
            return lambda E, o=out, a=a, b=b, op=op: E.tensor_tensor(o, a, b, op)

        def TS(out, a, s1, s2, op0, op1=None):
            if op1 is None:
                return lambda E, o=out, a=a, s1=s1, op0=op0: E.tensor_scalar(o, a, s1, None, op0)
            return lambda E, o=out, a=a, s1=s1, s2=s2, op0=op0, op1=op1: E.tensor_scalar(o, a, s1, s2, op0, op1)

        def ACT(out, a, f, bias=0.0, scale=1.0):
            return lambda E, o=out, a=a, f=f, b=bias, s=scale: E.activation(o, a, f, bias=b, scale=s)

        def CPY(out, a):
            return lambda E, o=out, a=a: E.copy(o, a)

        def VCP(out, a):
            return lambda E, o=out, a=a: E.tensor_copy(o, a)

        def STT(out, a, scol, b, op0, op1):
            return lambda E, o=out, a=a, s=scol, b=b, op0=op0, op1=op1: E.scalar_tensor_tensor(o, a, s, b, op0, op1)

        def DMA(out, in_):
            return lambda E, o=out, i=in_: E.dma_start(out=o, in_=i)

        def TRN(out, a):
            return lambda E, o=out, a=a: E.transpose(o, a, identb[:, :])

        def SCAN(out, d0, d1):
            return lambda E, o=out, d0=d0, d1=d1: E.tensor_tensor_scan(o, d0, d1, 0.0, ALU.mult, ALU.add)

        p.doi("gpsimd", DMA(wq_s[:, :].rearrange("p (k q) -> p k q", k=NK),
                            wq_sl.rearrange("(k p) q -> p k q", p=128)), lw, 16)
        p.doi("gpsimd", DMA(wk_s[:, :].rearrange("p (k q) -> p k q", k=NK),
                            wk_sl.rearrange("(k p) q -> p k q", p=128)), lw, 16)
        p.doi("gpsimd", DMA(wv_s[:, :].rearrange("p (k q) -> p k q", k=NK),
                            wv_sl.rearrange("(k p) q -> p k q", p=128)), lw, 16)
        p.doi("gpsimd", DMA(bqr[:, :], bq_sl), lw, 16)
        p.doi("gpsimd", DMA(bkr[:, :], bk_sl), lw, 16)
        p.doi("gpsimd", DMA(bvr[:, :], bv_sl), lw, 16)
        v_wld = p.cur(lw)
        p.doi("sync", DMA(mx16[:, :], mask16x), ld, 16)
        p.doi("sync", DMA(bmask[:, :], bandm), ld, 16)
        v_mx = p.cur(ld)
        p.do("vector", lambda E: E.memset(zcol[:, :], 0.0))
        p.do("vector", lambda E: E.memset(Ssb[:, :], 0.0))
        v_szero = p.doi("vector", lambda E: E.memset(Rf[:, :], 0.0), dv)
        p.w("tensor", lw, v_wld)
        p.w("tensor", ld, v_mx)
        p.w("tensor", dv, v_szero)
        p.w("vector", ld, v_mx)

        g_xn_free = [0, 0]
        g_qr_free = [0, 0]
        g_vts_free = [0, 0]
        g_ostg_free = [0, 0]
        g_kf_free = 0
        g_vt_free = 0
        g_q_free = 0
        g_chain_done = 0
        g_o2mm_prev = 0
        g_tp_free = 0
        g_w_free = 0
        g_p_free = 0
        g_o_free = 0
        g_o2_free = 0
        g_elem_free_dv = [0, 0]
        g_oevac = 0
        g_elem_free = [0, 0]

        for gi in range(gla_groups):
            bufp = gi % 2
            j = gi // 4
            col0 = (gi % 4) * 512
            xb0 = bufp * NK * 512
            xv = [xnt[:, xb0 + k * 512:xb0 + (k + 1) * 512] for k in range(NK)]
            vts_g = vts[:, bufp * 512:(bufp + 1) * 512]
            p.w("sync", cc, 1)
            p.w("sync", pe, g_xn_free[bufp])
            v_ld = p.doi("sync", DMA(
                xnt[:, xb0:xb0 + NK * 512].rearrange("p (k t) -> p k t", k=NK),
                gx_a[j * DM:(j + 1) * DM, col0:col0 + 512].rearrange("(k p) t -> p k t", p=128)), ld, 16)
            p.w("tensor", ld, v_ld)
            p.w("tensor", dv, g_kf_free)
            for k in range(NK):
                p.do("tensor", MM(kf_ps[:, :], wk_s[:, k * 128:(k + 1) * 128], xv[k], k == 0, False))
            v_kf = p.doi("tensor", MM(kf_ps[:, :], bkr[:, :], onerow[:, :], False, True), pe)
            p.w("tensor", sc, g_vt_free)
            for B in range(4):
                ts = slice(B * 128, (B + 1) * 128)
                for k in range(NK):
                    p.do("tensor", MM(vt_ps[:, ts], xv[k][:, ts], wv_s[:, k * 128:(k + 1) * 128],
                                      B == 0 and k == 0, False))
                if B < 3:
                    p.do("tensor", MM(vt_ps[:, ts], onerow[:, 0:128], bvr[:, :], False, False))
            v_vt = p.doi("tensor", MM(vt_ps[:, 384:512], onerow[:, 0:128], bvr[:, :], False, True), pe)
            p.w("scalar", pe, v_vt)
            p.w("scalar", pe, g_vts_free[bufp])
            g_vt_free = p.doi("scalar", CPY(vts_g, vt_ps[:, :]), sc)
            for h in range(4):
                p.w("tensor", sc, g_q_free)
                p.w("tensor", dv, g_qr_free[bufp])
                for k in range(NK):
                    p.do("tensor", MM(qps[:, :], wq_s[:, k * 512 + h * 128:k * 512 + (h + 1) * 128],
                                      xv[k], k == 0, False))
                v_q = p.doi("tensor", MM(qps[:, :], bqr[:, h * 128:(h + 1) * 128], onerow[:, :], False, True), pe)
                p.w("scalar", pe, v_q)
                g_q_free = p.doi("scalar", ACT(qr[:, (bufp * 4 + h) * 512:(bufp * 4 + h + 1) * 512],
                                               qps[:, :], AF.Relu, 0.0, QSCALE), sc)
            v_qr_grp = g_q_free

            if gla_upto == 30 and gi == 0:
                pass
            for B in range(4):
                if gla_upto < 2:
                    continue
                gb = gi * 4 + B
                pb = gb % 2
                Bs = slice(B * 128, (B + 1) * 128)
                kfsb, t_abs, t_rk, t_sp, t_g, t_cf, t_kf = (fpt(pb, i) for i in range(7))
                t_z16, t_z16b = fpt(pb, 7), fpt(pb, 8)
                t_z32, t_z32e = fpt(pb, 9), fpt(pb, 10)
                e16p, e16m, e16bm, e32p, e32em = (fpt(pb, 11 + i) for i in range(5))
                wsb = fpt(pb, 16, 4)
                kzd, kzx, kz32, kz32T = (bpt(pb, i) for i in range(4))
                qz16 = bpt(pb, 4, 4)
                qz32 = fpt(pb, 20, 4)
                pm1 = bpt(pb, 12, 4)
                pm2 = bpt(pb, 16, 4)
                kztb = bpt(pb, 20, 4)
                p1sb = bpt(pb, 24, 4)
                p2sb = bpt(pb, 28, 4)
                osb = bpt(pb, 32, 4)
                o2sb = bpt(pb, 36, 4)
                dl = dlp[:, pb * 8:pb * 8 + 4]
                ls = dlp[:, pb * 8 + 4:pb * 8 + 8]
                qrh = [qr[:, (bufp * 4 + h) * 512 + B * 128:(bufp * 4 + h) * 512 + (B + 1) * 128]
                       for h in range(4)]
                p.w("scalar", pe, v_kf)
                p.w("scalar", dv, g_elem_free_dv[pb])
                p.w("scalar", pe, g_elem_free[pb])
                v_kfsb = p.doi("scalar", CPY(kfsb, kf_ps[:, Bs]), sc)
                p.do("scalar", ACT(t_abs, kfsb, AF.Abs))
                p.do("scalar", ACT(t_abs, t_abs, AF.Exp, 0.0, -1.0))
                v_sp = p.doi("scalar", ACT(t_sp, t_abs, AF.Ln, 1.0, 1.0), sc)
                p.w("vector", sc, v_sp)
                p.w("vector", pe, g_elem_free[pb])
                p.do("vector", TS(t_rk, kfsb, 0.0, None, ALU.max))
                p.do("vector", TT(t_g, t_rk, t_sp, ALU.add))
                p.do("vector", TS(t_g, t_g, -1.0, None, ALU.mult))
                p.do("vector", TS(t_g, t_g, LN05, None, ALU.max))
                v_cfv = p.doi("vector", SCAN(t_cf, onesf[:, :], t_g), dv)
                p.w("scalar", dv, v_cfv)
                v_e2 = p.doi("scalar", ACT(t_g, t_g, AF.Exp), sc)
                p.w("vector", sc, v_e2)
                p.do("vector", TS(t_kf, t_g, -1.0, 1.0, ALU.mult, ALU.add))
                p.w("vector", dv, v_cfv)
                for jj in range(8):
                    js = slice(jj * 16, (jj + 1) * 16)
                    bcol = zcol[:, 0:1] if jj == 0 else t_cf[:, jj * 16 - 1:jj * 16]
                    ncol = t_cf[:, jj * 16 + 15:jj * 16 + 16]
                    p.do("vector", TS(t_z16[:, js], t_cf[:, js], bcol, None, ALU.subtract))
                    p.do("vector", TS(t_z16b[:, js], t_cf[:, js], ncol, None, ALU.subtract))
                for ii in range(4):
                    bs2 = slice(ii * 32, (ii + 1) * 32)
                    scol = zcol[:, 0:1] if ii == 0 else t_cf[:, ii * 32 - 1:ii * 32]
                    ecol = t_cf[:, ii * 32 + 31:ii * 32 + 32]
                    p.do("vector", TS(t_z32[:, bs2], t_cf[:, bs2], scol, None, ALU.subtract))
                    p.do("vector", TS(t_z32e[:, bs2], t_cf[:, bs2], ecol, None, ALU.subtract))
                p.do("vector", VCP(dl[:, 0:1], t_cf[:, 31:32]))
                v_z = p.doi("vector", TT(dl[:, 1:4], t_cf[:, 63:128:32], t_cf[:, 31:96:32], ALU.subtract), dv)
                p.w("scalar", dv, v_z)
                p.do("scalar", ACT(e16p, t_z16, AF.Exp))
                p.do("scalar", ACT(e16m, t_z16, AF.Exp, 0.0, -1.0))
                p.do("scalar", ACT(e16bm, t_z16b, AF.Exp, 0.0, -1.0))
                p.do("scalar", ACT(e32p, t_z32, AF.Exp))
                p.do("scalar", ACT(e32em, t_z32e, AF.Exp, 0.0, -1.0))
                v_ls = p.doi("scalar", ACT(ls, dl, AF.Exp), sc)
                v_ez = v_ls
                p.w("vector", sc, v_ez)
                p.do("vector", TT(kzd, t_kf, e16m, ALU.mult))
                p.do("vector", TT(kzx, t_kf, e16bm, ALU.mult))
                v_kz = p.doi("vector", TT(kz32, t_kf, e32em, ALU.mult), dv)
                p.w("vector", sc, v_qr_grp)
                for h in range(4):
                    p.do("vector", TT(qz16[:, h * 128:(h + 1) * 128], qrh[h], e16p, ALU.mult))
                qz32bm = qz32.rearrange("p (i h t) -> p i h t", i=4, h=4)
                for h in range(3):
                    p.do("vector", TT(qz32bm[:, :, h, :], qrh[h], e32p, ALU.mult))
                v_qz = p.doi("vector", TT(qz32bm[:, :, 3, :], qrh[3], e32p, ALU.mult), dv)
                if B == 3:
                    g_kf_free = v_kfsb * 0 + p.cur(dv)
                    g_qr_free[bufp] = v_qz
                if gla_upto < 3:
                    continue
                p.w("tensor", dv, v_kz)
                p.w("tensor", sc, g_tp_free)
                v_tp = p.doi("tensor", TRN(tp_view, kz32), pe)
                p.w("scalar", pe, v_tp)
                g_tp_free = p.doi("scalar", CPY(kz32T, tp_view), sc)
                if gla_upto < 4:
                    continue
                p.w("vector", sc, g_tp_free)
                for ii in range(3):
                    p.do("vector", TS(kztb[:, ii * 128:(ii + 1) * 128], kz32T, bmask[:, ii:ii + 1],
                                      None, ALU.mult))
                v_kzb = p.doi("vector", TS(kztb[:, 384:512], kz32T, bmask[:, 3:4],
                                           None, ALU.mult), dv)
                if gla_upto < 5:
                    continue
                p.w("tensor", dv, v_kzb)
                p.w("tensor", sc, g_w_free)
                p.w("tensor", sc, g_vt_free)
                for ii in range(3):
                    p.do("tensor", MM(w_ps[:, ii * 128:(ii + 1) * 128],
                                      kztb[:, ii * 128:(ii + 1) * 128], vts_g[:, Bs], ii == 0, False))
                v_W = p.doi("tensor", MM(w_ps[:, 384:512], kztb[:, 384:512], vts_g[:, Bs],
                                         False, True), pe)
                p.w("scalar", pe, v_W)
                p.w("scalar", dv, g_chain_done)
                g_w_free = p.doi("scalar", CPY(wsb, w_ps[:, :]), sc)
                if gla_upto < 6:
                    continue
                p.w("tensor", dv, v_qz)
                p.w("tensor", sc, g_p_free)
                for h in range(3):
                    hs = slice(h * 128, (h + 1) * 128)
                    p.do("tensor", MM(p1_ps[:, hs], kzd, qz16[:, hs], h == 0, False))
                v_P1 = p.doi("tensor", MM(p1_ps[:, 384:512], kzd, qz16[:, 384:512], False, True), pe)
                p.w("scalar", pe, v_P1)
                v_p1c = p.doi("scalar", CPY(p1sb, p1_ps[:, :]), sc)
                p.w("tensor", sc, v_p1c)
                for h in range(3):
                    hs = slice(h * 128, (h + 1) * 128)
                    p.do("tensor", MM(p1_ps[:, hs], kzx, qz16[:, hs], h == 0, False))
                v_P2 = p.doi("tensor", MM(p1_ps[:, 384:512], kzx, qz16[:, 384:512], False, True), pe)
                p.w("scalar", pe, v_P2)
                g_p_free = p.doi("scalar", CPY(p2sb, p1_ps[:, :]), sc)
                p.w("vector", sc, g_p_free)
                for h in range(4):
                    hs = slice(h * 128, (h + 1) * 128)
                    p.do("vector", TT(pm1[:, hs], p1sb[:, hs], m16[:, :], ALU.mult))
                for h in range(3):
                    hs = slice(h * 128, (h + 1) * 128)
                    p.do("vector", TT(pm2[:, hs], p2sb[:, hs], mx16[:, :], ALU.mult))
                v_pm = p.doi("vector", TT(pm2[:, 384:512], p2sb[:, 384:512], mx16[:, :], ALU.mult), dv)
                if gla_upto < 7:
                    continue
                Scur = Ssb[:, pb * 128:(pb + 1) * 128]
                Sprev = Ssb[:, (1 - pb) * 128:(2 - pb) * 128]
                p.w("vector", sc, g_w_free)
                p.w("vector", sc, v_ls)
                p.w("vector", pe, g_o2mm_prev)
                chain_src = [Sprev, Rf[:, 0:128], Rf[:, 128:256], Rf[:, 256:384]]
                chain_dst = [Rf[:, 0:128], Rf[:, 128:256], Rf[:, 256:384], Scur]
                for ii in range(4):
                    vci = p.doi("vector", STT(chain_dst[ii], chain_src[ii], ls[:, ii:ii + 1],
                                              wsb[:, ii * 128:(ii + 1) * 128], ALU.mult, ALU.add), dv)
                g_chain_done = vci
                p.w("tensor", sc, g_o_free)
                p.w("tensor", dv, v_pm)
                for h in range(4):
                    hs = slice(h * 128, (h + 1) * 128)
                    p.do("tensor", MM(o_ps[:, hs], vts_g[:, Bs], pm1[:, hs], h == 0, False))
                for h in range(4):
                    hs = slice(h * 128, (h + 1) * 128)
                    p.do("tensor", MM(o_ps[:, hs], vts_g[:, Bs], pm2[:, hs], False, h == 3))
                p.w("tensor", sc, g_o2_free)
                p.w("tensor", dv, g_chain_done)
                o2lhs = [Sprev, Rf[:, 0:128], Rf[:, 128:256], Rf[:, 256:384]]
                for ii in range(4):
                    mmfn = MM(o2_ps[:, ii * 128:(ii + 1) * 128], o2lhs[ii],
                              qz32[:, ii * 128:(ii + 1) * 128], ii == 0, ii == 3)
                    if ii == 3:
                        v_o = p.doi("tensor", mmfn, pe)
                    else:
                        p.do("tensor", mmfn)
                g_o2mm_prev = v_o
                g_elem_free[pb] = v_o
                p.w("scalar", pe, v_o)
                g_o_free = p.doi("scalar", CPY(osb, o_ps[:, :]), sc)
                g_o2_free = p.doi("scalar", CPY(
                    o2sb.rearrange("p (h c) -> p h c", h=4),
                    o2_ps[:, :].rearrange("p (i h t) -> p h i t", i=4, h=4)), sc)
                p.w("vector", sc, g_o2_free)
                p.w("vector", ld, g_ostg_free[bufp])
                g_oevac = p.doi("vector", TT(
                    ostg[:, bufp * 2048:(bufp + 1) * 2048].rearrange("p (h c) -> p h c", h=4)[:, :, Bs],
                    osb.rearrange("p (h c) -> p h c", h=4),
                    o2sb.rearrange("p (h c) -> p h c", h=4), ALU.add), dv)
                g_elem_free_dv[pb] = g_oevac
            if gla_upto < 7:
                g_xn_free[bufp] = p.cur(pe)
                continue
            p.w("sync", dv, g_oevac)
            g_ostg_free[bufp] = p.doi("sync", DMA(
                cc_b[j * 512:(j + 1) * 512, col0:col0 + 512].rearrange("(h e) t -> e h t", e=128),
                ostg[:, bufp * 2048:(bufp + 1) * 2048].rearrange("p (h t) -> p h t", h=4)), ld, 16)
            g_xn_free[bufp] = p.cur(pe)
            g_vts_free[bufp] = p.cur(pe)
        p.w("gpsimd", ld, p.cur(ld))
        p.doi("gpsimd", lambda E: E.collective_compute(
            "AllGather", ALU.bypass, replica_groups=GROUPS,
            ins=[cc_b.ap()], outs=[gx_b.ap()]), cc, 1)
        esGP.close()
        esG.close()

    def emit_postres(src_dram, gather, tag, final=False):
        esR = ExitStack()
        Ar = _alloc(esR, nc, f"bufA_{tag}", [128, NK * TPC], BF16)
        Arv = [Ar[:, k * TPC:(k + 1) * TPC] for k in range(NK)]
        a_tr = _alloc(esR, nc, f"a_t_{tag}", [128, TPC], F32)
        bp_tr = _alloc(esR, nc, f"bp_t_{tag}", [128, TPC], F32)
        vtmpr = _alloc(esR, nc, f"vtmp_{tag}", [128, 512], F32)
        vtmp2r = _alloc(esR, nc, f"vtmp2_{tag}", [128, 512], F32)
        x2bufr = _alloc(esR, nc, f"x2buf_{tag}", [128, NK * 512], BF16)
        fout = _alloc(esR, nc, f"fout_{tag}", [128, 2 * TPC], F32) if final else None
        esRS = ExitStack()
        s1r = esRS.enter_context(nc.psum_tensor(f"s1_{tag}", [128, 512], F32))
        s2r = esRS.enter_context(nc.psum_tensor(f"s2_{tag}", [128, 512], F32))
        v_ld0 = p.cur(ld)
        v_lw0 = p.cur(lw)
        p.barrier(bar)
        for e in ENGINES:
            p.w(e, ld, v_ld0)
            p.w(e, lw, v_lw0)
        p.w("gpsimd", cc, p.cur(cc))
        from concourse.bass_core_rust import IndirectOffsetOnAxis
        for k in range(NK):
            if gather:
                p.doi("gpsimd", lambda E, k=k, src=src_dram: E.indirect_dma_start(
                    out=Arv[k], out_offset=None, in_=src.ap(),
                    in_offset=IndirectOffsetOnAxis(oidx_s[:, k:k + 1], 0)), lw, 16)
            else:
                p.doi("gpsimd", lambda E, k=k, src=src_dram: E.dma_start(
                    out=Arv[k], in_=src[k * 128:(k + 1) * 128, :]), lw, 16)
        v_gat = p.cur(lw)
        for e in ("vector", "scalar", "tensor"):
            p.w(e, lw, v_gat)
        ln_scalars(Arv, a_tr, bp_tr, vtmpr, vtmp2r, x2bufr, s1r, s2r)
        v_ln = p.cur(dv)
        p.w("vector", dv, v_ln)
        last = None
        for k in range(NK):
            for tt in range(4):
                ts = slice(tt * 512, (tt + 1) * 512)
                p.do("vector", TTg(Arv[k][:, ts], Arv[k][:, ts], a_tr[:, ts], ALU.mult))
                p.do("vector", TTg(Arv[k][:, ts], Arv[k][:, ts], bp_tr[:, ts], ALU.subtract))
                dst = fout[:, (k % 2) * TPC:(k % 2) * TPC + TPC][:, ts] if final else Xv[k][:, ts]
                fn = TTg(dst, Xv[k][:, ts], Arv[k][:, ts], ALU.add)
                if final:
                    if (k, tt) == (0, 0):
                        st_basef = p.cur(st)
                    p.do("vector", fn)
                    if tt == 3:
                        if k >= 2:
                            p.w("vector", st, st_basef + (k - 1) * 16)
                        v_f = p.doi("vector", lambda E: E.nop(), dv)
                else:
                    if (k, tt) == (NK - 1, 3):
                        last = p.doi("vector", fn, dv)
                    else:
                        p.do("vector", fn)
        if final:
            pass
        esRS.close()
        esR.close()
        return p.cur(dv)

    def TTg(out, a, b, op):
        return lambda E, o=out, a=a, b=b, op=op: E.tensor_tensor(o, a, b, op)

    if nstage >= 4:
        emit_postres(gx_b, True, "r0")

    def emit_ffn(layer, wg_d, wu_d, wd_d, gs_d, us_d, ds_d, h_d, ffo_d, tag):
        dequant = (layer == 1)
        esF0 = ExitStack()
        al0 = lambda n, sh, dt: esF0.enter_context(nc.sbuf_tensor(f"{n}_{tag}", sh, dt))
        gs_s = al0("gs_s", [128, NI], F32)
        us_s = al0("us_s", [128, NI], F32)
        ds_s = al0("ds_s", [128, NK], F32)
        srow = al0("srow", [1, 8 * 512], BF16)
        n128r = al0("n128r", [1, 128], BF16)
        esF = ExitStack()
        alf = lambda n, sh, dt: esF.enter_context(nc.sbuf_tensor(f"{n}_{tag}", sh, dt))
        XN = alf("xn", [128, NK * TPC], BF16)
        XNv = [XN[:, k * TPC:(k + 1) * TPC] for k in range(NK)]
        a_tf = alf("a_tf", [128, TPC], F32)
        bp_tf = alf("bp_tf", [128, TPC], F32)
        vtmpf = alf("vtmpf", [128, 512], F32)
        vtmp2f = alf("vtmp2f", [128, 512], F32)
        x2buff = alf("x2buff", [128, NK * 512], BF16)
        wgu = alf("wgu", [128, 2 * 2 * NK * 128], BF16)
        gu_sb = alf("gu_sb", [128, 2 * 2 * 512], BF16)
        h_sb = alf("h_sb", [128, 2 * TPC], BF16)
        esFS = ExitStack()
        psf = lambda n, sh: esFS.enter_context(nc.psum_tensor(f"{n}_{tag}", sh, F32))
        zg = psf("zg", [128, 2 * 512])
        zu = psf("zu", [128, 2 * 512])
        smh = psf("smh", [128, 4 * 512])
        dps = [zg[:, 0:512], zg[:, 512:1024], zu[:, 0:512], zu[:, 512:1024]]

        def wguv(par, gu, k):
            return wgu[:, ((par * 2 + gu) * NK + k) * 128:((par * 2 + gu) * NK + k + 1) * 128]

        v_ld0 = p.cur(ld)
        v_lw0 = p.cur(lw)
        p.barrier(bar)
        for e in ENGINES:
            p.w(e, ld, v_ld0)
            p.w(e, lw, v_lw0)
        p.doi("sync", DMA(gs_s[:, :], gs_d), ld, 16)
        p.doi("sync", DMA(us_s[:, :], us_d), ld, 16)
        v_scl = p.doi("sync", DMA(ds_s[:, :], ds_d), ld, 16)
        p.do("vector", lambda E: E.memset(n128r[:, :], -128.0))
        wready = [0] * NI
        for par, i in ((0, 0), (1, 1)):
            p.doi("gpsimd", DMA(
                wgu[:, (par * 2) * NK * 128:(par * 2 + 1) * NK * 128].rearrange("p (k q) -> p k q", k=NK),
                wg_d[:, i * 128:(i + 1) * 128].rearrange("(k p) q -> p k q", p=128)), lw, 16)
            wready[i] = p.doi("gpsimd", DMA(
                wgu[:, (par * 2 + 1) * NK * 128:(par * 2 + 2) * NK * 128].rearrange("p (k q) -> p k q", k=NK),
                wu_d[:, i * 128:(i + 1) * 128].rearrange("(k p) q -> p k q", p=128)), lw, 16)
        ln_scalars(Xv, a_tf, bp_tf, vtmpf, vtmp2f, x2buff, zg[:, 0:512], zg[:, 512:1024])
        v_lnf = p.cur(dv)
        p.w("vector", dv, v_lnf)
        for k in range(NK):
            for tt in range(4):
                ts = slice(tt * 512, (tt + 1) * 512)
                p.do("vector", TTg(XNv[k][:, ts], Xv[k][:, ts], a_tf[:, ts], ALU.mult))
                if (k, tt) != (NK - 1, 3):
                    p.do("vector", TTg(XNv[k][:, ts], XNv[k][:, ts], bp_tf[:, ts], ALU.subtract))
        _l = slice(3 * 512, 4 * 512)
        v_xnf = p.doi("vector", TTg(XNv[NK - 1][:, _l], XNv[NK - 1][:, _l], bp_tf[:, _l], ALU.subtract), dv)
        v_sxn = p.cur(sc)
        if dequant:
            p.w("tensor", dv, v_xnf)
            for tt in range(4):
                ts = slice(tt * 512, (tt + 1) * 512)
                for k in range(NK):
                    p.do("tensor", MM(smh[:, tt * 512:(tt + 1) * 512], ones128[:, :],
                                      XNv[k][:, ts], k == 0, k == NK - 1))
            v_mm = p.doi("tensor", lambda E: E.drain(), pe)
            p.w("scalar", pe, v_mm)
            for tt in range(3):
                p.do("scalar", CPY(srow[:, tt * 512:(tt + 1) * 512],
                                   smh[0:1, tt * 512:(tt + 1) * 512]))
            v_sxn = p.doi("scalar", CPY(srow[:, 3 * 512:4 * 512], smh[0:1, 3 * 512:4 * 512]), sc)
        p.w("tensor", ld, v_scl)
        p.w("tensor", dv, v_xnf)
        p.w("scalar", ld, v_scl)
        g_zfree = [max(p.cur(sc), v_sxn)] * 2
        g_gufree = [0, 0]
        g_wfree = [0, 0]
        g_hfree = [0, 0]
        v_hrow = [0, 0, 0, 0]
        for i in range(NI):
            par = i % 2
            p.w("tensor", lw, wready[i])
            for tt in range(4):
                ts = slice(tt * 512, (tt + 1) * 512)
                zpar = (i * 4 + tt) % 2
                zgv = zg[:, zpar * 512:(zpar + 1) * 512]
                zuv = zu[:, zpar * 512:(zpar + 1) * 512]
                p.w("tensor", sc, g_zfree[zpar])
                for k in range(NK - 1):
                    p.do("tensor", MM(zgv, wguv(par, 0, k), XNv[k][:, ts], k == 0, False))
                if dequant:
                    p.do("tensor", MM(zgv, wguv(par, 0, NK - 1), XNv[NK - 1][:, ts], False, False))
                    p.w("tensor", sc, v_sxn)
                    p.do("tensor", MM(zgv, n128r[:, :], srow[:, tt * 512:(tt + 1) * 512], False, True))
                else:
                    p.do("tensor", MM(zgv, wguv(par, 0, NK - 1), XNv[NK - 1][:, ts], False, True))
                for k in range(NK - 1):
                    p.do("tensor", MM(zuv, wguv(par, 1, k), XNv[k][:, ts], k == 0, False))
                if dequant:
                    p.do("tensor", MM(zuv, wguv(par, 1, NK - 1), XNv[NK - 1][:, ts], False, False))
                    v_gu = p.doi("tensor", MM(zuv, n128r[:, :], srow[:, tt * 512:(tt + 1) * 512], False, True), pe)
                else:
                    v_gu = p.doi("tensor", MM(zuv, wguv(par, 1, NK - 1), XNv[NK - 1][:, ts], False, True), pe)
                gv = gu_sb[:, (zpar * 2) * 512:(zpar * 2 + 1) * 512]
                uv = gu_sb[:, (zpar * 2 + 1) * 512:(zpar * 2 + 2) * 512]
                p.w("scalar", pe, v_gu)
                p.w("scalar", dv, g_gufree[zpar])
                p.do("scalar", lambda E, o=gv, a=zgv, i=i: E.activation(
                    o, a, AF.Silu, bias=0.0, scale=gs_s[:, i:i + 1]))
                vsc = p.doi("scalar", lambda E, o=uv, a=zuv, i=i: E.activation(
                    o, a, AF.Copy, bias=0.0, scale=us_s[:, i:i + 1]), sc)
                g_zfree[zpar] = vsc
                p.w("vector", sc, vsc)
                p.w("vector", ld, g_hfree[par])
                vdv = p.doi("vector", TTg(h_sb[:, par * TPC + tt * 512:par * TPC + (tt + 1) * 512],
                                          gv, uv, ALU.mult), dv)
                g_gufree[zpar] = vdv
                if dequant:
                    p.w("tensor", dv, vdv)
                    if i == 0:
                        p.w("tensor", sc, v_sxn)
                    v_hrow[tt] = p.doi("tensor", MM(
                        smh[:, tt * 512:(tt + 1) * 512], ones128[:, :],
                        h_sb[:, par * TPC + tt * 512:par * TPC + (tt + 1) * 512],
                        i == 0, i == NI - 1), pe)
            g_wfree[par] = p.cur(pe)
            if i + 2 < NI:
                p.w("gpsimd", pe, g_wfree[par])
                p.doi("gpsimd", DMA(
                    wgu[:, (par * 2) * NK * 128:(par * 2 + 1) * NK * 128].rearrange("p (k q) -> p k q", k=NK),
                    wg_d[:, (i + 2) * 128:(i + 3) * 128].rearrange("(k p) q -> p k q", p=128)), lw, 16)
                wready[i + 2] = p.doi("gpsimd", DMA(
                    wgu[:, (par * 2 + 1) * NK * 128:(par * 2 + 2) * NK * 128].rearrange("p (k q) -> p k q", k=NK),
                    wu_d[:, (i + 2) * 128:(i + 3) * 128].rearrange("(k p) q -> p k q", p=128)), lw, 16)
            p.w("sync", dv, p.cur(dv))
            g_hfree[par] = p.doi("sync", DMA(
                h_d[i * 128:(i + 1) * 128, :], h_sb[:, par * TPC:(par + 1) * TPC]), ld, 16)
        v_smh = p.cur(sc)
        if dequant:
            for tt in range(4):
                p.w("scalar", pe, v_hrow[tt])
                if tt < 3:
                    p.do("scalar", CPY(srow[:, (4 + tt) * 512:(5 + tt) * 512],
                                       smh[0:1, tt * 512:(tt + 1) * 512]))
            v_smh = p.doi("scalar", CPY(srow[:, 7 * 512:8 * 512], smh[0:1, 3 * 512:4 * 512]), sc)
        v_hlast = p.cur(ld)
        v_pe0 = p.cur(pe)
        p.barrier(bar)
        for e in ENGINES:
            p.w(e, ld, v_hlast)
            p.w(e, lw, p.cur(lw))
        esFS.close()
        esF.close()
        esD = ExitStack()
        wd_s = esD.enter_context(nc.sbuf_tensor(f"wd_{tag}", [128, NI * 4 * 128], BF16))
        h_t = esD.enter_context(nc.sbuf_tensor(f"ht_{tag}", [128, 2 * 22 * 512], BF16))
        fo_sb = esD.enter_context(nc.sbuf_tensor(f"fo_{tag}", [128, 4 * TPC], BF16))
        esDS = ExitStack()
        dn_ps = esDS.enter_context(nc.psum_tensor(f"dn_{tag}", [128, 4 * 512], F32))
        dpsv = [dn_ps[:, d * 512:(d + 1) * 512] for d in range(4)]
        g_wdfree = 0
        g_htfree = [0, 0]
        g_dfree = [0, 0, 0, 0]
        for pp in range(4):
            p.w("gpsimd", pe, g_wdfree)
            v_wd = p.doi("gpsimd", DMA(
                wd_s[:, :].rearrange("p (i q) -> p i q", i=NI),
                wd_d[:, pp * 512:(pp + 1) * 512].rearrange("(i p) q -> p i q", p=128)), lw, 16)
            p.w("tensor", lw, v_wd)
            for tt in range(4):
                ts = slice(tt * 512, (tt + 1) * 512)
                for hh in range(2):
                    p.w("sync", pe, g_htfree[hh])
                    v_ht = p.doi("sync", DMA(
                        h_t[:, hh * 22 * 512:(hh + 1) * 22 * 512].rearrange("p (i q) -> p i q", i=22),
                        h_d[hh * 22 * 128:(hh + 1) * 22 * 128, ts].rearrange("(i p) q -> p i q", p=128)), ld, 16)
                    p.w("tensor", ld, v_ht)
                    for d in range(4):
                        if hh == 0:
                            p.w("tensor", sc, g_dfree[d])
                        for i in range(22):
                            ii = hh * 22 + i
                            first = (hh == 0 and i == 0)
                            last = (hh == 1 and i == 21)
                            wtile = wd_s[:, (ii * 4 + d) * 128:(ii * 4 + d + 1) * 128]
                            htile = h_t[:, (hh * 22 + i) * 512:(hh * 22 + i + 1) * 512]
                            if last:
                                if dequant:
                                    p.do("tensor", MM(dpsv[d], wtile, htile, False, False))
                                    p.w("tensor", sc, v_smh)
                                    p.doi("tensor", MM(dpsv[d], n128r[:, :],
                                                       srow[:, (4 + tt) * 512:(5 + tt) * 512],
                                                       False, True), pe)
                                else:
                                    p.doi("tensor", MM(dpsv[d], wtile, htile, False, True), pe)
                            elif hh == 0 and i == 21:
                                p.doi("tensor", MM(dpsv[d], wtile, htile, first, False), pe)
                            else:
                                p.do("tensor", MM(dpsv[d], wtile, htile, first, False))
                    g_htfree[hh] = p.cur(pe)
                v_dall = p.cur(pe)
                p.w("scalar", pe, v_dall)
                for d in range(4):
                    dout = pp * 4 + d
                    g_dfree[d] = p.doi("scalar", lambda E, d=d, dout=dout, ts=ts: E.activation(
                        fo_sb[:, d * TPC:(d + 1) * TPC][:, ts],
                        dpsv[d], AF.Copy, bias=0.0, scale=ds_s[:, dout:dout + 1]), sc)
            p.w("sync", sc, p.cur(sc))
            p.doi("sync", DMA(
                ffo_d[pp * 512:(pp + 1) * 512, :].rearrange("(d p) t -> p d t", p=128),
                fo_sb[:, :].rearrange("p (d t) -> p d t", d=4)), ld, 16)
            g_wdfree = p.cur(pe)
        p.w("gpsimd", ld, p.cur(ld))
        esDS.close()
        esD.close()
        esF0.close()

    if nstage >= 5:
        emit_ffn(0, gw_t, uw_t, dw_t, gs_r, us_r, ds_r, h_dram, ffo_dram, "f0")
    if nstage >= 6:
        emit_postres(ffo_dram, False, "r1")

    if nstage >= 7:
        esP7 = ExitStack()
        A7 = _alloc(esP7, nc, "bufA7", [128, NK * TPC], BF16)
        A7v = [A7[:, k * TPC:(k + 1) * TPC] for k in range(NK)]
        a_t7 = _alloc(esP7, nc, "a_t7", [128, TPC], F32)
        bp_t7 = _alloc(esP7, nc, "bp_t7", [128, TPC], F32)
        vtmp7 = _alloc(esP7, nc, "vtmp7", [128, 512], F32)
        vtmp27 = _alloc(esP7, nc, "vtmp27", [128, 512], F32)
        x2buf7 = _alloc(esP7, nc, "x2buf7", [128, NK * 512], BF16)
        esPS7 = ExitStack()
        s17 = esPS7.enter_context(nc.psum_tensor("s17", [128, 512], F32))
        s27 = esPS7.enter_context(nc.psum_tensor("s27", [128, 512], F32))
        v_ld0 = p.cur(ld)
        v_lw0 = p.cur(lw)
        p.barrier(bar)
        for e in ENGINES:
            p.w(e, ld, v_ld0)
            p.w(e, lw, v_lw0)
        ln_scalars(Xv, a_t7, bp_t7, vtmp7, vtmp27, x2buf7, s17, s27)
        v_ln7 = p.cur(dv)
        p.w("vector", dv, v_ln7)
        for k in range(NK):
            for tt in range(4):
                ts = slice(tt * 512, (tt + 1) * 512)
                p.do("vector", TTg(A7v[k][:, ts], Xv[k][:, ts], a_t7[:, ts], ALU.mult))
                if (k, tt) != (NK - 1, 3):
                    p.do("vector", TTg(A7v[k][:, ts], A7v[k][:, ts], bp_t7[:, ts], ALU.subtract))
        _l7 = slice(3 * 512, 4 * 512)
        v_xn7 = p.doi("vector", TTg(A7v[NK - 1][:, _l7], A7v[NK - 1][:, _l7], bp_t7[:, _l7], ALU.subtract), dv)
        p.w("sync", dv, v_xn7)
        p.doi("sync", lambda E: E.dma_start(
            out=cc_c.ap().rearrange("(k p) t -> p k t", p=128),
            in_=A7[:, :].rearrange("p (k t) -> p k t", k=NK)), ld, 16)
        v_spill = p.doi("sync", lambda E: E.dma_start(
            out=xspill.ap().rearrange("(k p) t -> p k t", p=128),
            in_=X[:, :].rearrange("p (k t) -> p k t", k=NK)), ld, 16)
        p.w("gpsimd", ld, p.cur(ld))
        p.doi("gpsimd", lambda E: E.collective_compute(
            "AllGather", ALU.bypass, replica_groups=GROUPS,
            ins=[cc_c.ap()], outs=[gx_c.ap()]), cc, 1)
        v_cc7 = p.cur(cc)
        esPS7.close()
        esP7.close()

        esA = ExitStack()
        alA = lambda n, sh, dt: esA.enter_context(nc.sbuf_tensor(n, sh, dt))
        Kt = X
        Vt = alA("Vt", [128, 4 * 64 * 128], BF16)
        esA1 = ExitStack()
        wkv = esA1.enter_context(nc.sbuf_tensor("wkv", [128, 4 * NK * 128], BF16))
        xnt7 = esA1.enter_context(nc.sbuf_tensor("xnt7", [128, 2 * NK * 512], BF16))
        csb = esA1.enter_context(nc.sbuf_tensor("csb", [128, 2 * 512], F32))
        ssb = esA1.enter_context(nc.sbuf_tensor("ssb", [128, 2 * 512], F32))
        rot_s = esA1.enter_context(nc.sbuf_tensor("rot_s", [128, 128], BF16))
        rtmp = esA1.enter_context(nc.sbuf_tensor("rtmp", [128, 4 * 512], F32))
        krw = esA1.enter_context(nc.sbuf_tensor("krw", [128, 4 * 512], BF16))
        esAS = ExitStack()
        kv_ps = esAS.enter_context(nc.psum_tensor("kv_ps", [128, 2 * 512], F32))
        rp_ps = esAS.enter_context(nc.psum_tensor("rp_ps", [128, 512], F32))

        p.doi("sync", DMA(rot_s[:, :], rot_t), ld, 16)
        p.barrier(bar)
        for e in ENGINES:
            p.w(e, ld, v_spill)
        p.w("gpsimd", cc, v_cc7)
        p.w("sync", cc, v_cc7)

        def kv_pass(which):
            g_xf = [p.cur(pe), p.cur(pe)]
            g_kvfree = [p.cur(sc), p.cur(sc)]
            g_csfree = [0, 0]
            g_krw_dv = [0, 0]
            base = 1024 if which else 512
            p.w("gpsimd", pe, p.cur(pe))
            for h in range(4):
                p.doi("gpsimd", DMA(
                    wkv[:, h * NK * 128:(h + 1) * NK * 128].rearrange("p (k q) -> p k q", k=NK),
                    wqkv_sl[:, base + h * 128:base + (h + 1) * 128].rearrange("(k p) q -> p k q", p=128)), lw, 16)
            v_wh = p.cur(lw)
            p.w("tensor", lw, v_wh)
            for tt in range(16):
                par = tt % 2
                j, c0 = tt // 4, (tt % 4) * 512
                p.w("sync", pe, g_xf[par])
                v_xnld = p.doi("sync", DMA(
                    xnt7[:, par * NK * 512:(par + 1) * NK * 512].rearrange("p (k t) -> p k t", k=NK),
                    gx_c[j * DM:(j + 1) * DM, c0:c0 + 512].rearrange("(k p) t -> p k t", p=128)), ld, 16)
                if which == 1:
                    p.w("sync", dv, g_csfree[par])
                    p.doi("sync", DMA(csb[:, par * 512:(par + 1) * 512],
                                      cos_d[:, tt * 512:(tt + 1) * 512]), ld, 16)
                    v_csld = p.doi("sync", DMA(ssb[:, par * 512:(par + 1) * 512],
                                               sin_d[:, tt * 512:(tt + 1) * 512]), ld, 16)
                xv7 = [xnt7[:, (par * NK + k) * 512:(par * NK + k + 1) * 512] for k in range(NK)]
                p.w("tensor", ld, v_xnld if which == 0 else p.cur(ld))
                for h in range(4):
                    kpar = (tt * 4 + h) % 2
                    kvv = kv_ps[:, kpar * 512:(kpar + 1) * 512]
                    wv = [wkv[:, (h * NK + k) * 128:(h * NK + k + 1) * 128] for k in range(NK)]
                    p.w("tensor", sc, g_kvfree[kpar])
                    if which == 0:
                        for tc in range(4):
                            tsl = slice(tc * 128, (tc + 1) * 128)
                            for k in range(NK):
                                lastk = (tc == 3 and k == NK - 1)
                                mfn = MM(kvv[:, tsl], xv7[k][:, tsl], wv[k],
                                         tc == 0 and k == 0, lastk)
                                if lastk:
                                    v_kv = p.doi("tensor", mfn, pe)
                                else:
                                    p.do("tensor", mfn)
                        p.w("scalar", pe, v_kv)
                        g_kvfree[kpar] = p.doi("scalar", lambda E, h=h, tt=tt, kvv=kvv: E.copy(
                            Vt[:, :].rearrange("p (h c e) -> p h c e", h=4, c=64)[:, h, tt * 4:tt * 4 + 4, :],
                            kvv.rearrange("p (c e) -> p c e", c=4)), sc)
                    else:
                        rpar = (tt * 4 + h) % 2
                        kr0 = krw[:, rpar * 1024:rpar * 1024 + 512]
                        kr1 = krw[:, rpar * 1024 + 512:rpar * 1024 + 1024]
                        rt0 = rtmp[:, rpar * 1024:rpar * 1024 + 512]
                        rt1 = rtmp[:, rpar * 1024 + 512:rpar * 1024 + 1024]
                        for k in range(NK):
                            mfn = MM(kvv, wv[k], xv7[k], k == 0, k == NK - 1)
                            if k == NK - 1:
                                v_kv = p.doi("tensor", mfn, pe)
                            else:
                                p.do("tensor", mfn)
                        p.w("scalar", pe, v_kv)
                        p.w("scalar", dv, g_krw_dv[rpar])
                        v_raw = p.doi("scalar", CPY(kr0, kvv), sc)
                        p.w("tensor", sc, v_raw)
                        v_rot = p.doi("tensor", MM(rp_ps[:, :], rot_s[:, :], kr0, True, True), pe)
                        p.w("scalar", pe, v_rot)
                        v_rr = p.doi("scalar", CPY(kr1, rp_ps[:, :]), sc)
                        g_kvfree[kpar] = v_rr
                        p.w("vector", sc, v_rr)
                        p.w("vector", ld, v_csld)
                        p.do("vector", TTg(rt0, kr0, csb[:, par * 512:(par + 1) * 512], ALU.mult))
                        p.do("vector", TTg(rt1, kr1, ssb[:, par * 512:(par + 1) * 512], ALU.mult))
                        vk2 = p.doi("vector", TTg(Kt[:, (h * 16 + tt) * 512:(h * 16 + tt + 1) * 512],
                                                  rt0, rt1, ALU.add), dv)
                        g_krw_dv[rpar] = vk2
                        if h == 3:
                            g_csfree[par] = vk2
                g_xf[par] = p.cur(pe)
            return

        kv_pass(0)
        kv_pass(1)
        v_ld0 = p.cur(ld)
        v_lw0 = p.cur(lw)
        p.barrier(bar)
        for e in ENGINES:
            p.w(e, ld, v_ld0)
            p.w(e, lw, v_lw0)
        esAS.close()
        esA1.close()

        esA2 = ExitStack()
        alF = lambda n, sh, dt: esA2.enter_context(nc.sbuf_tensor(n, sh, dt))
        wq7 = alF("wq7", [128, 4 * NK * 128], BF16)
        xnq = alF("xnq", [128, NK * 512], BF16)
        cqb = alF("cqb", [128, 512], F32)
        sqb = alF("sqb", [128, 512], F32)
        rot_q = alF("rot_q", [128, 128], BF16)
        rtq = alF("rtq", [128, 2 * 512], F32)
        qrw = alF("qrw", [128, 2 * 512], BF16)
        qzt = alF("qzt", [128, 4 * 512], BF16)
        l1m_s = alF("l1m_s", [128, 4 * 512], BF16)
        pt_sb = alF("pt_sb", [128, 2 * 512], BF16)
        ebias = alF("ebias", [128, 1], F32)
        ob_sb = alF("ob_sb", [128, 512], BF16)
        rn_sb = alF("rn_sb", [128, 2 * 512], F32)
        ostg7 = alF("ostg7", [128, 2 * 4 * 512], BF16)
        esFS7 = ExitStack()
        q_ps = esFS7.enter_context(nc.psum_tensor("q_ps", [128, 512], F32))
        rq_ps = esFS7.enter_context(nc.psum_tensor("rq_ps", [128, 512], F32))
        s_ps = esFS7.enter_context(nc.psum_tensor("s_ps", [128, 2 * 512], F32))
        o7_ps2 = esFS7.enter_context(nc.psum_tensor("o7_ps", [128, 2 * 512], F32))
        n7_ps2 = esFS7.enter_context(nc.psum_tensor("n7_ps", [128, 2 * 512], F32))
        p.doi("sync", DMA(rot_q[:, :], rot_t), ld, 16)
        v_m7 = p.doi("sync", DMA(l1m_s[:, :], l1mask), ld, 16)
        p.w("tensor", ld, v_m7)
        p.w("vector", ld, v_m7)
        v_eb = p.doi("vector", lambda E: E.memset(ebias[:, :], EXP_SHIFT), dv)
        p.w("scalar", dv, v_eb)
        g_xq = [0, 0]
        g_cqfree = [0, 0]
        g_qps_free = 0
        g_rq_free = 0
        g_qz_free = 0
        g_sps_free = [0, 0]
        g_pt_free = [0, 0]
        g_pt_pe = [0, 0]
        g_o7_free = [0, 0]
        g_n7_free = [0, 0]
        g_ob_free = 0
        g_rn_free = 0
        g_ostg7_free = [0, 0]
        for h in range(4):
            v_wq7 = p.doi("gpsimd", DMA(
                wq7[:, h * NK * 128:(h + 1) * NK * 128].rearrange("p (k q) -> p k q", k=NK),
                wqkv_sl[:, h * 128:(h + 1) * 128].rearrange("(k p) q -> p k q", p=128)), lw, 16)
        for qt in range(16):
            par = qt % 2
            j, c0 = qt // 4, (qt % 4) * 512
            p.w("sync", pe, g_xq[0])
            v_xq = p.doi("sync", DMA(
                xnq[:, :].rearrange("p (k t) -> p k t", k=NK),
                gx_c[j * DM:(j + 1) * DM, c0:c0 + 512].rearrange("(k p) t -> p k t", p=128)), ld, 16)
            p.w("sync", dv, g_cqfree[0])
            p.doi("sync", DMA(cqb[:, :], cos_d[:, qt * 512:(qt + 1) * 512]), ld, 16)
            v_cq = p.doi("sync", DMA(sqb[:, :], sin_d[:, qt * 512:(qt + 1) * 512]), ld, 16)
            xvq = [xnq[:, k * 512:(k + 1) * 512] for k in range(NK)]
            nkc = 4 * qt + 4
            v_qz7h = [0, 0, 0, 0]
            for h in range(4):
                p.w("tensor", ld, v_xq)
                p.w("tensor", sc, g_qps_free)
                p.w("tensor", lw, v_wq7)
                for k in range(NK):
                    mfn = MM(q_ps[:, :], wq7[:, (h * NK + k) * 128:(h * NK + k + 1) * 128],
                             xvq[k], k == 0, k == NK - 1)
                    if k == NK - 1:
                        v_q = p.doi("tensor", mfn, pe)
                    else:
                        p.do("tensor", mfn)
                p.w("scalar", pe, v_q)
                p.w("scalar", dv, g_qz_free)
                g_qps_free = p.doi("scalar", CPY(qrw[:, 0:512], q_ps[:, :]), sc)
                p.w("tensor", sc, g_qps_free)
                p.w("tensor", sc, g_rq_free)
                v_rotq = p.doi("tensor", MM(rq_ps[:, :], rot_q[:, :], qrw[:, 0:512], True, True), pe)
                p.w("scalar", pe, v_rotq)
                g_rq_free = p.doi("scalar", CPY(qrw[:, 512:1024], rq_ps[:, :]), sc)
                p.w("vector", sc, g_rq_free)
                p.w("vector", ld, v_cq)
                p.do("vector", TTg(rtq[:, 0:512], qrw[:, 0:512], cqb[:, :], ALU.mult))
                p.do("vector", TTg(rtq[:, 512:1024], qrw[:, 512:1024], sqb[:, :], ALU.mult))
                v_qz7h[h] = p.doi("vector", TTg(qzt[:, h * 512:(h + 1) * 512],
                                                rtq[:, 0:512], rtq[:, 512:1024], ALU.add), dv)
                g_qz_free = v_qz7h[h]
            g_cqfree[0] = v_qz7h[3]
            for h in range(4):
                hpar = h % 2
                o7_ps = o7_ps2[:, hpar * 512:(hpar + 1) * 512]
                n7_ps = n7_ps2[:, hpar * 512:(hpar + 1) * 512]
                p.w("tensor", dv, v_qz7h[h])

                def emit_score(kc):
                    spar = kc % 2
                    sv = s_ps[:, spar * 512:(spar + 1) * 512]
                    p.w("tensor", sc, g_sps_free[spar])
                    ktile = Kt[:, :].rearrange("p (h q) -> p h q", h=4)[:, h, kc * 128:(kc + 1) * 128]
                    return p.doi("tensor", MM(sv, ktile, qzt[:, h * 512:(h + 1) * 512], True, True), pe)

                v_s = {0: emit_score(0)}
                for kc in range(nkc):
                    spar = kc % 2
                    sv = s_ps[:, spar * 512:(spar + 1) * 512]
                    p.w("scalar", pe, v_s[kc])
                    p.w("scalar", dv, g_pt_free[spar])
                    p.w("scalar", pe, g_pt_pe[spar])
                    vexp = p.doi("scalar", lambda E, sv=sv, spar=spar: E.activation(
                        pt_sb[:, spar * 512:(spar + 1) * 512], sv, AF.Exp,
                        bias=ebias[:, 0:1], scale=QSCALE), sc)
                    g_sps_free[spar] = vexp
                    diag = kc - 4 * qt
                    if diag >= 0:
                        p.w("vector", sc, vexp)
                        vexp_d = p.doi("vector", TTg(pt_sb[:, spar * 512:(spar + 1) * 512],
                                                     pt_sb[:, spar * 512:(spar + 1) * 512],
                                                     l1m_s[:, diag * 512:(diag + 1) * 512], ALU.mult), dv)
                        g_pt_free[spar] = vexp_d
                    if kc + 1 < nkc:
                        v_s[kc + 1] = emit_score(kc + 1)
                    if diag >= 0:
                        p.w("tensor", dv, vexp_d)
                    else:
                        p.w("tensor", sc, vexp)
                    vtile = Vt[:, :].rearrange("p (h c e) -> p h c e", h=4, c=64)[:, h, kc, :]
                    if kc == 0:
                        p.w("tensor", sc, g_o7_free[hpar])
                        p.w("tensor", sc, g_n7_free[hpar])
                    p.do("tensor", MM(o7_ps, vtile, pt_sb[:, spar * 512:(spar + 1) * 512],
                                      kc == 0, kc == nkc - 1))
                    v_on = p.doi("tensor", MM(n7_ps, ones128[:, :],
                                              pt_sb[:, spar * 512:(spar + 1) * 512],
                                              kc == 0, kc == nkc - 1), pe)
                    g_pt_pe[spar] = v_on
                p.w("scalar", pe, v_on)
                p.w("scalar", dv, g_rn_free)
                p.do("scalar", lambda E, n7_ps=n7_ps: E.activation(rn_sb[:, 0:512], n7_ps, AF.Ln))
                v_rn = p.doi("scalar", lambda E: E.activation(rn_sb[:, 512:1024], rn_sb[:, 0:512],
                                                              AF.Exp, bias=0.0, scale=-1.0), sc)
                g_n7_free[hpar] = v_rn
                p.w("scalar", dv, g_ob_free)
                v_ob = p.doi("scalar", lambda E, o7_ps=o7_ps: E.copy(ob_sb[:, :], o7_ps), sc)
                g_o7_free[hpar] = v_ob
                p.w("vector", sc, v_ob)
                p.w("vector", ld, g_ostg7_free[par])
                v_of = p.doi("vector", TTg(ostg7[:, (par * 4 + h) * 512:(par * 4 + h + 1) * 512],
                                           ob_sb[:, :], rn_sb[:, 512:1024], ALU.mult), dv)
                g_ob_free = v_of
                g_rn_free = v_of
            p.w("sync", dv, p.cur(dv))
            g_ostg7_free[par] = p.doi("sync", DMA(
                cc_d[j * 512:(j + 1) * 512, c0:c0 + 512].rearrange("(h e) t -> e h t", e=128),
                ostg7[:, par * 2048:(par + 1) * 2048].rearrange("p (h t) -> p h t", h=4)), ld, 16)
            g_xq[0] = p.cur(pe)
        p.w("gpsimd", ld, p.cur(ld))
        p.doi("gpsimd", lambda E: E.collective_compute(
            "AllGather", ALU.bypass, replica_groups=GROUPS,
            ins=[cc_d.ap()], outs=[gx_d.ap()]), cc, 1)
        esFS7.close()
        esA2.close()
        esA.close()

        v_ld0 = p.cur(ld)
        v_lw0 = p.cur(lw)
        p.barrier(bar)
        for e in ENGINES:
            p.w(e, ld, v_ld0)
            p.w(e, lw, v_lw0)
        p.doi("gpsimd", lambda E: E.dma_start(
            out=X[:, :].rearrange("p (k t) -> p k t", k=NK),
            in_=xspill.ap().rearrange("(k p) t -> p k t", p=128)), lw, 16)
        esO = ExitStack()
        OF = esO.enter_context(nc.sbuf_tensor("OF", [128, NK * TPC], BF16))
        OFv = [OF[:, k * TPC:(k + 1) * TPC] for k in range(NK)]
        wo_s = esO.enter_context(nc.sbuf_tensor("wo_s", [128, 2 * NK * 128], BF16))
        ao_sb = esO.enter_context(nc.sbuf_tensor("ao_sb", [128, 2 * TPC], BF16))
        esOS = ExitStack()
        op_ps = esOS.enter_context(nc.psum_tensor("op_ps", [128, 2 * 512], F32))
        from concourse.bass_core_rust import IndirectOffsetOnAxis
        p.w("gpsimd", cc, p.cur(cc))
        gatld = {}
        for k in range(NK):
            gatld[k] = p.doi("gpsimd", lambda E, k=k: E.indirect_dma_start(
                out=OFv[k], out_offset=None, in_=gx_d.ap(),
                in_offset=IndirectOffsetOnAxis(oidx_s[:, k:k + 1], 0)), lw, 16)
        g_wofree = [0, 0]
        g_aofree = [0, 0]
        g_opfree = [0, 0]
        woldv = {}
        for dk in (0, 1):
            woldv[dk] = p.doi("gpsimd", DMA(
                wo_s[:, (dk % 2) * NK * 128:((dk % 2) + 1) * NK * 128].rearrange("p (k q) -> p k q", k=NK),
                wout_t[:, dk * 128:(dk + 1) * 128].rearrange("(k p) q -> p k q", p=128)), lw, 16)
        for dk in range(NK):
            wpar = dk % 2
            p.w("tensor", lw, woldv[dk])
            p.w("tensor", lw, gatld[NK - 1])
            for tt in range(4):
                ts = slice(tt * 512, (tt + 1) * 512)
                opar = (dk * 4 + tt) % 2
                opv = op_ps[:, opar * 512:(opar + 1) * 512]
                p.w("tensor", sc, g_opfree[opar])
                for k in range(NK):
                    mfn = MM(opv, wo_s[:, (wpar * NK + k) * 128:(wpar * NK + k + 1) * 128],
                             OFv[k][:, ts], k == 0, k == NK - 1)
                    if k == NK - 1:
                        v_op = p.doi("tensor", mfn, pe)
                    else:
                        p.do("tensor", mfn)
                p.w("scalar", pe, v_op)
                p.w("scalar", ld, g_aofree[dk % 2])
                g_opfree[opar] = p.doi("scalar", CPY(
                    ao_sb[:, (dk % 2) * TPC:(dk % 2) * TPC + TPC][:, ts], opv), sc)
            if dk + 2 < NK:
                p.w("gpsimd", pe, p.cur(pe))
                woldv[dk + 2] = p.doi("gpsimd", DMA(
                    wo_s[:, wpar * NK * 128:(wpar + 1) * NK * 128].rearrange("p (k q) -> p k q", k=NK),
                    wout_t[:, (dk + 2) * 128:(dk + 3) * 128].rearrange("(k p) q -> p k q", p=128)), lw, 16)
            p.w("sync", sc, p.cur(sc))
            g_aofree[dk % 2] = p.doi("sync", DMA(
                a2o_dram[dk * 128:(dk + 1) * 128, :],
                ao_sb[:, (dk % 2) * TPC:(dk % 2) * TPC + TPC]), ld, 16)
        p.w("gpsimd", ld, p.cur(ld))
        esOS.close()
        esO.close()

    if nstage >= 8:
        emit_postres(a2o_dram, False, "r2")
    if nstage >= 9:
        emit_ffn(1, qg_t, qu_t, qd_t, sg1_r, su1_r, sd1_r, h2_dram, ffo2_dram, "f1")
    if nstage >= 10:
        emit_postres(ffo2_dram, False, "r3")

    xout = sb("xout", [128, 2 * TPC], F32)
    dbg16 = sb("dbg16", [128, 2 * TPC], BF16)

    def dump16(views, dest):
        st_base = p.cur(st)
        for k in range(NK):
            buf = xout[:, (k % 2) * TPC:(k % 2 + 1) * TPC]
            p.w("scalar", st, st_base + (k - 1) * 16)
            v_up = p.doi("scalar", lambda E, k=k, buf=buf: E.activation(buf, views[k], AF.Copy), sc)
            p.w("sync", sc, v_up)
            p.doi("sync", lambda E, k=k, buf=buf, dest=dest: E.dma_start(
                out=dest[k * 128:(k + 1) * 128, :], in_=buf), st, 16)

    p.w("scalar", lw, 16)
    p.w("scalar", dv, p.cur(dv))
    p.w("scalar", cc, p.cur(cc))
    p.w("gpsimd", ld, p.cur(ld))
    p.w("gpsimd", cc, p.cur(cc))
    dump16(Xv, o_sl)
    if dbg_what == "xn":
        dump16(Av, dbg)
    if dbg_what in ("ccb", "gxb"):
        from concourse.bass_core_rust import IndirectOffsetOnAxis
        st_base = p.cur(st)
        v_ups = {}
        for k in range(NK):
            buf16 = dbg16[:, (k % 2) * 2048:(k % 2 + 1) * 2048]
            fbuf = xout[:, (k % 2) * TPC:(k % 2 + 1) * TPC]
            if k >= 2:
                p.w("gpsimd", sc, v_ups[k - 2])
            if dbg_what == "ccb":
                v_l = p.doi("gpsimd", lambda E, b=buf16, k=k: E.dma_start(
                    out=b, in_=cc_b[k * 128:(k + 1) * 128, :]), lw, 16)
            else:
                v_l = p.doi("gpsimd", lambda E, b=buf16, k=k: E.indirect_dma_start(
                    out=b, out_offset=None, in_=gx_b.ap(),
                    in_offset=IndirectOffsetOnAxis(oidx_s[:, k:k + 1], 0)), lw, 16)
            p.w("scalar", lw, v_l)
            p.w("scalar", st, st_base + (k - 1) * 16)
            v_ups[k] = p.doi("scalar", lambda E, f=fbuf, b=buf16: E.copy(f, b), sc)
            p.w("sync", sc, v_ups[k])
            p.doi("sync", lambda E, k=k, f=fbuf: E.dma_start(out=dbg[k * 128:(k + 1) * 128, :], in_=f), st, 16)
    p.w("sync", st, p.cur(st))

    if esP_open:
        esPS.close()
        esP.close()

    with p.es, nc.Block() as block:
        p.finalize(block)
    return nc


def _alloc(es, nc, name, shape, dt):
    return es.enter_context(nc.sbuf_tensor(name, shape, dt))



def _consts():
    ident = np.eye(128, dtype=np.float32).astype(BF)
    tau = np.arange(128)[:, None]
    t = np.arange(128)[None, :]
    mask16 = ((tau // 16 == t // 16) & (tau <= t)).astype(np.float32).astype(BF)
    mask16x = ((t // 16 == tau // 16 + 1) & (tau // 32 == t // 32)).astype(np.float32).astype(BF)
    l1m = np.zeros((128, 4, 512), np.float32)
    for o in range(4):
        l1m[:, o, :] = (np.arange(128)[:, None] + o * 128 <= np.arange(512)[None, :])
    l1m = l1m.reshape(128, 4 * 512).astype(BF)
    R = np.zeros((128, 128), np.float32)
    for i in range(32):
        R[2 * i, 2 * i + 1] = -1.0
        R[2 * i + 1, 2 * i] = 1.0
    rot_t = R.T.copy().astype(BF)
    inv_freq = 1.0 / 10000.0 ** (np.arange(0, 64, 2, dtype=np.float32) / 64)
    ang = np.arange(SEQ, dtype=np.float32)[:, None] * inv_freq[None, :]
    cs = np.repeat(np.cos(ang), 2, axis=1).T
    sn = np.repeat(np.sin(ang), 2, axis=1).T
    cos_d = np.concatenate([cs, np.ones((64, SEQ), np.float32)], axis=0)
    sin_d = np.concatenate([sn, np.zeros((64, SEQ), np.float32)], axis=0)
    return dict(ident_b=np.ascontiguousarray(ident), mask16=np.ascontiguousarray(mask16),
                mask16x=np.ascontiguousarray(mask16x),
                l1mask=np.ascontiguousarray(l1m), rot_t=np.ascontiguousarray(rot_t),
                cos_d=np.ascontiguousarray(cos_d), sin_d=np.ascontiguousarray(sin_d))


def make_in_maps(inputs):
    d = inputs
    x = np.asarray(d["L0_attn_hidden_states"])
    consts = _consts()
    shared = dict(
        gw_t=np.ascontiguousarray(np.asarray(d["L0_ffn_gate_w"]).T),
        uw_t=np.ascontiguousarray(np.asarray(d["L0_ffn_up_w"]).T),
        dw_t=np.ascontiguousarray(np.asarray(d["L0_ffn_down_w"]).T),
        gs_r=np.ascontiguousarray(np.asarray(d["L0_ffn_gate_s"]).reshape(NI, 128).T),
        us_r=np.ascontiguousarray(np.asarray(d["L0_ffn_up_s"]).reshape(NI, 128).T),
        ds_r=np.ascontiguousarray(np.asarray(d["L0_ffn_down_s"]).reshape(NK, 128).T),
        wout_t=np.ascontiguousarray(np.asarray(d["L1_attn_w_out"]).T),
        qg_t=np.ascontiguousarray(np.asarray(d["L1_ffn_wq_gate"]).T),
        qu_t=np.ascontiguousarray(np.asarray(d["L1_ffn_wq_up"]).T),
        qd_t=np.ascontiguousarray(np.asarray(d["L1_ffn_wq_down"]).T),
        sg1_r=np.ascontiguousarray(np.asarray(d["L1_ffn_s_gate"]).reshape(NI, 128).T),
        su1_r=np.ascontiguousarray(np.asarray(d["L1_ffn_s_up"]).reshape(NI, 128).T),
        sd1_r=np.ascontiguousarray(np.asarray(d["L1_ffn_s_down"]).reshape(NK, 128).T),
        **consts,
    )
    wq_t = np.ascontiguousarray(np.asarray(d["L0_attn_Wq"]).T)
    wk_t = np.ascontiguousarray(np.asarray(d["L0_attn_Wk"]).T)
    wv_t = np.ascontiguousarray(np.asarray(d["L0_attn_Wv"]).T)
    bq = np.asarray(d["L0_attn_bq"])
    bk = np.asarray(d["L0_attn_bk"])
    bv = np.asarray(d["L0_attn_bv"])
    wqkv_t = np.ascontiguousarray(np.asarray(d["L1_attn_w_qkv"]).T)
    in_maps = []
    for c in range(NCORES):
        b, g = c // 4, c % 4
        m = dict(shared)
        m["x_sl"] = np.ascontiguousarray(x[b, g * TPC:(g + 1) * TPC, :].T)
        m["wq_sl"] = np.ascontiguousarray(wq_t[:, 512 * g:512 * (g + 1)])
        m["wk_sl"] = np.ascontiguousarray(wk_t[:, 128 * g:128 * (g + 1)])
        m["wv_sl"] = np.ascontiguousarray(wv_t[:, 128 * g:128 * (g + 1)])
        m["bq_sl"] = np.ascontiguousarray(bq[512 * g:512 * (g + 1)].reshape(1, 512))
        m["bk_sl"] = np.ascontiguousarray(bk[128 * g:128 * (g + 1)].reshape(1, 128))
        m["bv_sl"] = np.ascontiguousarray(bv[128 * g:128 * (g + 1)].reshape(1, 128))
        m["wqkv_sl"] = np.ascontiguousarray(wqkv_t[:, 1536 * g:1536 * (g + 1)])
        kk = np.arange(NK)
        pp = np.arange(128)
        m["oidx"] = np.ascontiguousarray(
            ((kk[None, :] // 4) * 2048 + g * 512 + (kk[None, :] % 4) * 128 + pp[:, None]).astype(np.int32))
        bm = np.zeros((128, 4), np.float32)
        for i in range(4):
            bm[i * 32:(i + 1) * 32, i] = 1.0
        m["bandm"] = bm
        in_maps.append(m)
    return in_maps


def kernel(**inputs):
    in_maps = make_in_maps(inputs)
    nc = build()
    res = run_bass_kernel_spmd(nc, in_maps, core_ids=list(range(NCORES)))
    out = np.empty((2, SEQ, DM), np.float32)
    for c in range(NCORES):
        b, g = c // 4, c % 4
        out[b, g * TPC:(g + 1) * TPC, :] = res.results[c]["o_sl"].T
    return out


# version 08
# speedup vs baseline: 1.0308x; 1.0308x over previous
import sys

for _p in ("/opt/trn_rl_repo",):
    if _p not in sys.path:
        sys.path.insert(0, _p)

import numpy as np
import ml_dtypes
from contextlib import ExitStack

import concourse.bass as bass
import concourse.mybir as mybir
from concourse.bass_utils import run_bass_kernel_spmd

F32 = mybir.dt.float32
BF16 = mybir.dt.bfloat16
I32 = mybir.dt.int32
AF = mybir.ActivationFunctionType
ALU = mybir.AluOpType
BF = ml_dtypes.bfloat16

NCORES = 8
DM = 2048
TPC = 2048
NK = DM // 128
SEQ = 8192
INTER = 5632
NI = INTER // 128
GROUPS = [[0, 1, 2, 3], [4, 5, 6, 7]]
LN_EPS = 1e-5
LN05 = float(np.log(0.05))
QSCALE = 128.0 ** -0.5
EXP_SHIFT = -12.0

ENGINES = ("tensor", "vector", "scalar", "gpsimd", "sync")


class Prog:

    def __init__(self, nc):
        self.nc = nc
        self.q = {e: [] for e in ENGINES}
        self.semv = {}
        self.es = ExitStack()

    def sem(self, name):
        h = self.es.enter_context(self.nc.semaphore(name))
        self.semv[h] = 0
        return h

    def cur(self, s):
        return self.semv[s]

    def w(self, eng, s, val=None):
        v = self.semv[s] if val is None else val
        if v > 0:
            self.q[eng].append(lambda E, s=s, v=v: E.wait_ge(s, v))

    def do(self, eng, fn):
        self.q[eng].append(fn)

    def doi(self, eng, fn, s, amt=1):
        self.semv[s] += amt
        self.q[eng].append(lambda E, fn=fn, s=s, a=amt: fn(E).then_inc(s, a))
        return self.semv[s]

    def barrier(self, bar, engines=ENGINES):
        n = len(engines)
        base = self.semv[bar]
        for e in engines:
            self.doi(e, lambda E: E.drain(), bar, 1)
        for e in engines:
            self.w(e, bar, base + n)

    def finalize(self, block):
        for e in ENGINES:
            ops = self.q[e]
            getattr(block, e)(lambda E, ops=ops: [f(E) for f in ops])



def build(nstage=99, dbg_what=None, detect_races=False, gla_groups=16, gla_upto=7):
    nc = bass.Bass("TRN2", target_bir_lowering=False, debug=False,
                   detect_race_conditions=detect_races)
    D = nc.dram_tensor

    x_sl = D("x_sl", [DM, TPC], F32, kind="ExternalInput").ap()
    wq_sl = D("wq_sl", [DM, 512], F32, kind="ExternalInput").ap()
    wk_sl = D("wk_sl", [DM, 128], F32, kind="ExternalInput").ap()
    wv_sl = D("wv_sl", [DM, 128], F32, kind="ExternalInput").ap()
    bq_sl = D("bq_sl", [1, 512], F32, kind="ExternalInput").ap()
    bk_sl = D("bk_sl", [1, 128], F32, kind="ExternalInput").ap()
    bv_sl = D("bv_sl", [1, 128], F32, kind="ExternalInput").ap()
    gw_t = D("gw_t", [DM, INTER], F32, kind="ExternalInput").ap()
    uw_t = D("uw_t", [DM, INTER], F32, kind="ExternalInput").ap()
    dw_t = D("dw_t", [INTER, DM], F32, kind="ExternalInput").ap()
    gs_r = D("gs_r", [128, NI], F32, kind="ExternalInput").ap()
    us_r = D("us_r", [128, NI], F32, kind="ExternalInput").ap()
    ds_r = D("ds_r", [128, NK], F32, kind="ExternalInput").ap()
    wqkv_sl = D("wqkv_sl", [DM, 1536], F32, kind="ExternalInput").ap()
    wout_t = D("wout_t", [DM, DM], F32, kind="ExternalInput").ap()
    qg_t = D("qg_t", [DM, INTER], I32, kind="ExternalInput").ap()
    qu_t = D("qu_t", [DM, INTER], I32, kind="ExternalInput").ap()
    qd_t = D("qd_t", [INTER, DM], I32, kind="ExternalInput").ap()
    sg1_r = D("sg1_r", [128, NI], F32, kind="ExternalInput").ap()
    su1_r = D("su1_r", [128, NI], F32, kind="ExternalInput").ap()
    sd1_r = D("sd1_r", [128, NK], F32, kind="ExternalInput").ap()
    ident_b = D("ident_b", [128, 128], BF16, kind="ExternalInput").ap()
    mask16 = D("mask16", [128, 128], BF16, kind="ExternalInput").ap()
    mask16x = D("mask16x", [128, 128], BF16, kind="ExternalInput").ap()
    l1mask = D("l1mask", [128, 4 * 512], BF16, kind="ExternalInput").ap()
    rot_t = D("rot_t", [128, 128], BF16, kind="ExternalInput").ap()
    oidx = D("oidx", [128, NK], I32, kind="ExternalInput").ap()
    bandm = D("bandm", [128, 4], F32, kind="ExternalInput").ap()
    cos_d = D("cos_d", [128, SEQ], F32, kind="ExternalInput").ap()
    sin_d = D("sin_d", [128, SEQ], F32, kind="ExternalInput").ap()

    o_sl = D("o_sl", [DM, TPC], F32, kind="ExternalOutput").ap()
    dbg = D("dbg", [DM, TPC], F32, kind="ExternalOutput").ap() if dbg_what else None

    cc_a = D("cc_a", [DM, TPC], BF16)
    gx_a = D("gx_a", [4 * DM, TPC], BF16)
    cc_b = D("cc_b", [DM, TPC], BF16)
    gx_b = D("gx_b", [4 * DM, TPC], BF16)
    cc_c = D("cc_c", [DM, TPC], BF16)
    gx_c = D("gx_c", [4 * DM, TPC], BF16)
    cc_d = D("cc_d", [DM, TPC], BF16)
    gx_d = D("gx_d", [4 * DM, TPC], BF16)
    h_dram = D("h_dram", [INTER, TPC], BF16)
    h2_dram = D("h2_dram", [INTER, TPC], BF16)
    ffo_dram = D("ffo_dram", [DM, TPC], BF16)
    a2o_dram = D("a2o_dram", [DM, TPC], BF16)
    ffo2_dram = D("ffo2_dram", [DM, TPC], BF16)
    xspill = D("xspill", [DM, TPC], BF16)

    p = Prog(nc)
    pe = p.sem("pe")
    dv = p.sem("dv")
    sc = p.sem("sc")
    gp = p.sem("gp")
    ld = p.sem("ld")
    lw = p.sem("lw")
    st = p.sem("st")
    cc = p.sem("cc")
    bar = p.sem("bar")

    es = p.es
    sb = lambda name, shape, dt: es.enter_context(nc.sbuf_tensor(name, shape, dt))

    ones128 = sb("ones128", [128, 128], BF16)
    onesf = sb("onesf", [128, 128], F32)
    identb = sb("identb", [128, 128], BF16)
    m16 = sb("m16", [128, 128], BF16)
    onerow = sb("onerow", [1, 512], BF16)
    oidx_s = sb("oidx_s", [128, NK], I32)
    X = sb("X", [128, NK * TPC], BF16)

    Xv = [X[:, k * TPC:(k + 1) * TPC] for k in range(NK)]

    p.doi("gpsimd", lambda E: E.dma_start(
        out=X[:, :].rearrange("p (k t) -> p k t", k=NK),
        in_=x_sl.rearrange("(k p) t -> p k t", p=128)), lw, 16)
    p.do("vector", lambda E: E.memset(ones128[:, :], 1.0))
    p.do("vector", lambda E: E.memset(onesf[:, :], 1.0))
    v_dmac = p.doi("vector", lambda E: E.memset(onerow[:, :], 1.0), dv)
    p.doi("sync", lambda E: E.dma_start(out=identb[:, :], in_=ident_b), ld, 16)
    p.doi("sync", lambda E: E.dma_start(out=oidx_s[:, :], in_=oidx), ld, 16)
    p.doi("sync", lambda E: E.dma_start(out=m16[:, :], in_=mask16), ld, 16)
    p.w("tensor", dv, v_dmac)
    p.w("vector", lw, 16)

    def ln_scalars(src_views, a_t, bp_t, vtmp, vtmp2, x2buf, s1_ps, s2_ps):
        for tt in range(4):
            ts = slice(tt * 512, (tt + 1) * 512)
            if tt > 0:
                p.w("vector", pe, p.cur(pe))
            for k in range(NK - 1):
                p.do("vector", lambda E, k=k, ts=ts: E.tensor_tensor(
                    x2buf[:, k * 512:(k + 1) * 512], src_views[k][:, ts], src_views[k][:, ts], ALU.mult))
            v_x2 = p.doi("vector", lambda E, ts=ts: E.tensor_tensor(
                x2buf[:, (NK - 1) * 512:NK * 512], src_views[NK - 1][:, ts], src_views[NK - 1][:, ts], ALU.mult), dv)
            if tt > 0:
                p.w("tensor", sc, p.cur(sc))
            p.w("tensor", dv, v_x2)
            for k in range(NK):
                p.do("tensor", lambda E, k=k, ts=ts: E.matmul(
                    s1_ps[:, :], ones128[:, :], src_views[k][:, ts],
                    start=(k == 0), stop=(k == NK - 1)))
            for k in range(NK - 1):
                p.do("tensor", lambda E, k=k: E.matmul(
                    s2_ps[:, :], ones128[:, :], x2buf[:, k * 512:(k + 1) * 512],
                    start=(k == 0), stop=False))
            v_mm = p.doi("tensor", lambda E: E.matmul(
                s2_ps[:, :], ones128[:, :], x2buf[:, (NK - 1) * 512:NK * 512],
                start=False, stop=True), pe)
            p.w("scalar", pe, v_mm)
            p.do("scalar", lambda E, ts=ts: E.activation(bp_t[:, ts], s1_ps[:, :], AF.Copy, bias=0.0, scale=1.0 / DM))
            v_m = p.doi("scalar", lambda E: E.activation(vtmp[:, :], s2_ps[:, :], AF.Copy, bias=LN_EPS, scale=1.0 / DM), sc)
            p.w("vector", sc, v_m)
            p.do("vector", lambda E, ts=ts: E.tensor_tensor(vtmp2[:, :], bp_t[:, ts], bp_t[:, ts], ALU.mult))
            v_var = p.doi("vector", lambda E: E.tensor_tensor(vtmp[:, :], vtmp[:, :], vtmp2[:, :], ALU.subtract), dv)
            p.w("scalar", dv, v_var)
            p.do("scalar", lambda E: E.activation(vtmp2[:, :], vtmp[:, :], AF.Ln))
            v_a = p.doi("scalar", lambda E, ts=ts: E.activation(a_t[:, ts], vtmp2[:, :], AF.Exp, bias=0.0, scale=-0.5), sc)
            p.w("vector", sc, v_a)
            p.doi("vector", lambda E, ts=ts: E.tensor_tensor(bp_t[:, ts], bp_t[:, ts], a_t[:, ts], ALU.mult), dv)

    esP = ExitStack()
    A = _alloc(esP, nc, "bufA", [128, NK * TPC], BF16)
    Av = [A[:, k * TPC:(k + 1) * TPC] for k in range(NK)]
    a_t = _alloc(esP, nc, "a_t", [128, TPC], F32)
    bp_t = _alloc(esP, nc, "bp_t", [128, TPC], F32)
    vtmp = _alloc(esP, nc, "vtmp", [128, 512], F32)
    vtmp2 = _alloc(esP, nc, "vtmp2", [128, 512], F32)
    x2buf = _alloc(esP, nc, "x2buf", [128, NK * 512], BF16)
    esPS = ExitStack()
    s1_ps = esPS.enter_context(nc.psum_tensor("s1_ps", [128, 512], F32))
    s2_ps = esPS.enter_context(nc.psum_tensor("s2_ps", [128, 512], F32))

    if nstage >= 1:
        ln_scalars(Xv, a_t, bp_t, vtmp, vtmp2, x2buf, s1_ps, s2_ps)
        v_ln = p.cur(dv)
        p.w("vector", dv, v_ln)
        for k in range(NK):
            for tt in range(4):
                ts = slice(tt * 512, (tt + 1) * 512)
                p.do("vector", lambda E, k=k, ts=ts: E.tensor_tensor(Av[k][:, ts], Xv[k][:, ts], a_t[:, ts], ALU.mult))
                if (k, tt) != (NK - 1, 3):
                    p.do("vector", lambda E, k=k, ts=ts: E.tensor_tensor(Av[k][:, ts], Av[k][:, ts], bp_t[:, ts], ALU.subtract))
        _lts = slice(3 * 512, 4 * 512)
        v_xn = p.doi("vector", lambda E: E.tensor_tensor(
            Av[NK - 1][:, _lts], Av[NK - 1][:, _lts], bp_t[:, _lts], ALU.subtract), dv)
        if nstage >= 2:
            p.w("sync", dv, v_xn)
            p.doi("sync", lambda E: E.dma_start(
                out=cc_a.ap().rearrange("(k p) t -> p k t", p=128),
                in_=A[:, :].rearrange("p (k t) -> p k t", k=NK)), ld, 16)
            p.w("gpsimd", ld, p.cur(ld))
            p.doi("gpsimd", lambda E: E.collective_compute(
                "AllGather", ALU.bypass, replica_groups=GROUPS,
                ins=[cc_a.ap()], outs=[gx_a.ap()]), cc, 1)
    esP_open = True
    if dbg_what != "xn":
        esPS.close()
        esP.close()
        esP_open = False
    v_cca_ld = p.cur(ld)
    p.barrier(bar)
    p.w("vector", ld, v_cca_ld)
    p.w("gpsimd", ld, v_cca_ld)
    p.w("sync", ld, v_cca_ld)
    p.w("scalar", ld, v_cca_ld)
    p.w("tensor", ld, v_cca_ld)

    if nstage >= 3:
        esG = ExitStack()
        al = lambda n, s, d: esG.enter_context(nc.sbuf_tensor(n, s, d))
        wq_s = al("wq_s", [128, NK * 512], BF16)
        wk_s = al("wk_s", [128, NK * 128], BF16)
        wv_s = al("wv_s", [128, NK * 128], BF16)
        bqr = al("bqr", [1, 512], BF16)
        bkr = al("bkr", [1, 128], BF16)
        bvr = al("bvr", [1, 128], BF16)
        mx16 = al("mx16", [128, 128], BF16)
        bmask = al("bmask", [128, 4], F32)
        zcol = al("zcol", [128, 1], F32)
        xnt = al("xnt", [128, 2 * NK * 512], BF16)
        qr = al("qr", [128, 2 * 4 * 512], BF16)
        vts = al("vts", [128, 2 * 512], BF16)
        ostg = al("ostg", [128, 2 * 4 * 512], BF16)
        Ssb = al("Ssb", [128, 2 * 128], F32)
        Rf = al("Rf", [128, 3 * 128], F32)
        FP = al("FP", [128, 2 * 24 * 128], F32)
        BP = al("BP", [128, 2 * 40 * 128], BF16)
        dlp = al("dlp", [128, 2 * 8], F32)

        def fpt(pb, i, w=1):
            return FP[:, (pb * 24 + i) * 128:(pb * 24 + i + w) * 128]

        def bpt(pb, i, w=1):
            return BP[:, (pb * 40 + i) * 128:(pb * 40 + i + w) * 128]

        esGP = ExitStack()
        psA = lambda n, s, d=F32: esGP.enter_context(nc.psum_tensor(n, s, d))
        qps = psA("qps", [128, 512])
        kf_ps = psA("kf_ps", [128, 512])
        vt_ps = psA("vt_ps", [128, 512])
        w_ps = psA("w_ps", [128, 512])
        p1_ps = psA("p1_ps", [128, 512])
        o_ps = psA("o_ps", [128, 512])
        o2_ps = psA("o2_ps", [128, 512])
        tp_ps = psA("tp_ps", [128, 128], BF16)
        tp_view = tp_ps[:, :]

        def MM(out, l, r, start, stop, tp=None):
            if tp is None:
                return lambda E, o=out, l=l, r=r, s=start, t=stop: E.matmul(o, l, r, start=s, stop=t)
            return lambda E, o=out, l=l, r=r, s=start, t=stop, q=tp: E.matmul(o, l, r, start=s, stop=t, tile_position=q)

        def TT(out, a, b, op):
            return lambda E, o=out, a=a, b=b, op=op: E.tensor_tensor(o, a, b, op)

        def TS(out, a, s1, s2, op0, op1=None):
            if op1 is None:
                return lambda E, o=out, a=a, s1=s1, op0=op0: E.tensor_scalar(o, a, s1, None, op0)
            return lambda E, o=out, a=a, s1=s1, s2=s2, op0=op0, op1=op1: E.tensor_scalar(o, a, s1, s2, op0, op1)

        def ACT(out, a, f, bias=0.0, scale=1.0):
            return lambda E, o=out, a=a, f=f, b=bias, s=scale: E.activation(o, a, f, bias=b, scale=s)

        def CPY(out, a):
            return lambda E, o=out, a=a: E.copy(o, a)

        def VCP(out, a):
            return lambda E, o=out, a=a: E.tensor_copy(o, a)

        def STT(out, a, scol, b, op0, op1):
            return lambda E, o=out, a=a, s=scol, b=b, op0=op0, op1=op1: E.scalar_tensor_tensor(o, a, s, b, op0, op1)

        def DMA(out, in_):
            return lambda E, o=out, i=in_: E.dma_start(out=o, in_=i)

        def TRN(out, a):
            return lambda E, o=out, a=a: E.transpose(o, a, identb[:, :])

        def SCAN(out, d0, d1):
            return lambda E, o=out, d0=d0, d1=d1: E.tensor_tensor_scan(o, d0, d1, 0.0, ALU.mult, ALU.add)

        p.doi("gpsimd", DMA(wq_s[:, :].rearrange("p (k q) -> p k q", k=NK),
                            wq_sl.rearrange("(k p) q -> p k q", p=128)), lw, 16)
        p.doi("gpsimd", DMA(wk_s[:, :].rearrange("p (k q) -> p k q", k=NK),
                            wk_sl.rearrange("(k p) q -> p k q", p=128)), lw, 16)
        p.doi("gpsimd", DMA(wv_s[:, :].rearrange("p (k q) -> p k q", k=NK),
                            wv_sl.rearrange("(k p) q -> p k q", p=128)), lw, 16)
        p.doi("gpsimd", DMA(bqr[:, :], bq_sl), lw, 16)
        p.doi("gpsimd", DMA(bkr[:, :], bk_sl), lw, 16)
        p.doi("gpsimd", DMA(bvr[:, :], bv_sl), lw, 16)
        v_wld = p.cur(lw)
        p.doi("sync", DMA(mx16[:, :], mask16x), ld, 16)
        p.doi("sync", DMA(bmask[:, :], bandm), ld, 16)
        v_mx = p.cur(ld)
        p.do("vector", lambda E: E.memset(zcol[:, :], 0.0))
        p.do("vector", lambda E: E.memset(Ssb[:, :], 0.0))
        v_szero = p.doi("vector", lambda E: E.memset(Rf[:, :], 0.0), dv)
        p.w("tensor", lw, v_wld)
        p.w("tensor", ld, v_mx)
        p.w("tensor", dv, v_szero)
        p.w("vector", ld, v_mx)

        g_xn_free = [0, 0]
        g_qr_free = [0, 0]
        g_vts_free = [0, 0]
        g_ostg_free = [0, 0]
        g_kf_free = 0
        g_vt_free = 0
        g_q_free = 0
        g_chain_done = 0
        g_o2mm_prev = 0
        g_tp_free = 0
        g_w_free = 0
        g_p_free = 0
        g_o_free = 0
        g_o2_free = 0
        g_elem_free_dv = [0, 0]
        g_oevac = 0
        g_elem_free = [0, 0]

        for gi in range(gla_groups):
            bufp = gi % 2
            j = gi // 4
            col0 = (gi % 4) * 512
            xb0 = bufp * NK * 512
            xv = [xnt[:, xb0 + k * 512:xb0 + (k + 1) * 512] for k in range(NK)]
            vts_g = vts[:, bufp * 512:(bufp + 1) * 512]
            p.w("sync", cc, 1)
            p.w("sync", pe, g_xn_free[bufp])
            v_ld = p.doi("sync", DMA(
                xnt[:, xb0:xb0 + NK * 512].rearrange("p (k t) -> p k t", k=NK),
                gx_a[j * DM:(j + 1) * DM, col0:col0 + 512].rearrange("(k p) t -> p k t", p=128)), ld, 16)
            p.w("tensor", ld, v_ld)
            p.w("tensor", dv, g_kf_free)
            for k in range(NK):
                p.do("tensor", MM(kf_ps[:, :], wk_s[:, k * 128:(k + 1) * 128], xv[k], k == 0, False))
            v_kf = p.doi("tensor", MM(kf_ps[:, :], bkr[:, :], onerow[:, :], False, True), pe)
            p.w("tensor", sc, g_vt_free)
            for B in range(4):
                ts = slice(B * 128, (B + 1) * 128)
                for k in range(NK):
                    p.do("tensor", MM(vt_ps[:, ts], xv[k][:, ts], wv_s[:, k * 128:(k + 1) * 128],
                                      B == 0 and k == 0, False))
                if B < 3:
                    p.do("tensor", MM(vt_ps[:, ts], onerow[:, 0:128], bvr[:, :], False, False))
            v_vt = p.doi("tensor", MM(vt_ps[:, 384:512], onerow[:, 0:128], bvr[:, :], False, True), pe)
            p.w("scalar", pe, v_vt)
            p.w("scalar", pe, g_vts_free[bufp])
            g_vt_free = p.doi("scalar", CPY(vts_g, vt_ps[:, :]), sc)
            for h in range(4):
                p.w("tensor", sc, g_q_free)
                p.w("tensor", dv, g_qr_free[bufp])
                for k in range(NK):
                    p.do("tensor", MM(qps[:, :], wq_s[:, k * 512 + h * 128:k * 512 + (h + 1) * 128],
                                      xv[k], k == 0, False))
                v_q = p.doi("tensor", MM(qps[:, :], bqr[:, h * 128:(h + 1) * 128], onerow[:, :], False, True), pe)
                p.w("scalar", pe, v_q)
                g_q_free = p.doi("scalar", ACT(qr[:, (bufp * 4 + h) * 512:(bufp * 4 + h + 1) * 512],
                                               qps[:, :], AF.Relu, 0.0, QSCALE), sc)
            v_qr_grp = g_q_free

            if gla_upto == 30 and gi == 0:
                pass
            for B in range(4):
                if gla_upto < 2:
                    continue
                gb = gi * 4 + B
                pb = gb % 2
                Bs = slice(B * 128, (B + 1) * 128)
                kfsb, t_abs, t_rk, t_sp, t_g, t_cf, t_kf = (fpt(pb, i) for i in range(7))
                t_z16, t_z16b = fpt(pb, 7), fpt(pb, 8)
                t_z32, t_z32e = fpt(pb, 9), fpt(pb, 10)
                e16p, e16m, e16bm, e32p, e32em = (fpt(pb, 11 + i) for i in range(5))
                wsb = fpt(pb, 16, 4)
                kzd, kzx, kz32, kz32T = (bpt(pb, i) for i in range(4))
                qz16 = bpt(pb, 4, 4)
                qz32 = fpt(pb, 20, 4)
                pm1 = bpt(pb, 12, 4)
                pm2 = bpt(pb, 16, 4)
                kztb = bpt(pb, 20, 4)
                p1sb = bpt(pb, 24, 4)
                p2sb = bpt(pb, 28, 4)
                osb = bpt(pb, 32, 4)
                o2sb = bpt(pb, 36, 4)
                dl = dlp[:, pb * 8:pb * 8 + 4]
                ls = dlp[:, pb * 8 + 4:pb * 8 + 8]
                qrh = [qr[:, (bufp * 4 + h) * 512 + B * 128:(bufp * 4 + h) * 512 + (B + 1) * 128]
                       for h in range(4)]
                p.w("scalar", pe, v_kf)
                p.w("scalar", dv, g_elem_free_dv[pb])
                p.w("scalar", pe, g_elem_free[pb])
                v_kfsb = p.doi("scalar", CPY(kfsb, kf_ps[:, Bs]), sc)
                p.do("scalar", ACT(t_abs, kfsb, AF.Abs))
                p.do("scalar", ACT(t_abs, t_abs, AF.Exp, 0.0, -1.0))
                v_sp = p.doi("scalar", ACT(t_sp, t_abs, AF.Ln, 1.0, 1.0), sc)
                p.w("vector", sc, v_sp)
                p.w("vector", pe, g_elem_free[pb])
                p.do("vector", TS(t_rk, kfsb, 0.0, None, ALU.max))
                p.do("vector", TT(t_g, t_rk, t_sp, ALU.add))
                p.do("vector", TS(t_g, t_g, -1.0, None, ALU.mult))
                p.do("vector", TS(t_g, t_g, LN05, None, ALU.max))
                v_cfv = p.doi("vector", SCAN(t_cf, onesf[:, :], t_g), dv)
                p.w("scalar", dv, v_cfv)
                v_e2 = p.doi("scalar", ACT(t_g, t_g, AF.Exp), sc)
                p.w("vector", sc, v_e2)
                p.do("vector", TS(t_kf, t_g, -1.0, 1.0, ALU.mult, ALU.add))
                p.w("vector", dv, v_cfv)
                for jj in range(8):
                    js = slice(jj * 16, (jj + 1) * 16)
                    bcol = zcol[:, 0:1] if jj == 0 else t_cf[:, jj * 16 - 1:jj * 16]
                    ncol = t_cf[:, jj * 16 + 15:jj * 16 + 16]
                    p.do("vector", TS(t_z16[:, js], t_cf[:, js], bcol, None, ALU.subtract))
                    p.do("vector", TS(t_z16b[:, js], t_cf[:, js], ncol, None, ALU.subtract))
                for ii in range(4):
                    bs2 = slice(ii * 32, (ii + 1) * 32)
                    scol = zcol[:, 0:1] if ii == 0 else t_cf[:, ii * 32 - 1:ii * 32]
                    ecol = t_cf[:, ii * 32 + 31:ii * 32 + 32]
                    p.do("vector", TS(t_z32[:, bs2], t_cf[:, bs2], scol, None, ALU.subtract))
                    p.do("vector", TS(t_z32e[:, bs2], t_cf[:, bs2], ecol, None, ALU.subtract))
                p.do("vector", VCP(dl[:, 0:1], t_cf[:, 31:32]))
                v_z = p.doi("vector", TT(dl[:, 1:4], t_cf[:, 63:128:32], t_cf[:, 31:96:32], ALU.subtract), dv)
                p.w("scalar", dv, v_z)
                p.do("scalar", ACT(e16p, t_z16, AF.Exp))
                p.do("scalar", ACT(e16m, t_z16, AF.Exp, 0.0, -1.0))
                p.do("scalar", ACT(e16bm, t_z16b, AF.Exp, 0.0, -1.0))
                p.do("scalar", ACT(e32p, t_z32, AF.Exp))
                p.do("scalar", ACT(e32em, t_z32e, AF.Exp, 0.0, -1.0))
                v_ls = p.doi("scalar", ACT(ls, dl, AF.Exp), sc)
                v_ez = v_ls
                p.w("vector", sc, v_ez)
                p.do("vector", TT(kzd, t_kf, e16m, ALU.mult))
                p.do("vector", TT(kzx, t_kf, e16bm, ALU.mult))
                v_kz = p.doi("vector", TT(kz32, t_kf, e32em, ALU.mult), dv)
                p.w("vector", sc, v_qr_grp)
                for h in range(4):
                    p.do("vector", TT(qz16[:, h * 128:(h + 1) * 128], qrh[h], e16p, ALU.mult))
                qz32bm = qz32.rearrange("p (i h t) -> p i h t", i=4, h=4)
                for h in range(3):
                    p.do("vector", TT(qz32bm[:, :, h, :], qrh[h], e32p, ALU.mult))
                v_qz = p.doi("vector", TT(qz32bm[:, :, 3, :], qrh[3], e32p, ALU.mult), dv)
                if B == 3:
                    g_kf_free = v_kfsb * 0 + p.cur(dv)
                    g_qr_free[bufp] = v_qz
                if gla_upto < 3:
                    continue
                p.w("tensor", dv, v_kz)
                p.w("tensor", sc, g_tp_free)
                v_tp = p.doi("tensor", TRN(tp_view, kz32), pe)
                p.w("scalar", pe, v_tp)
                g_tp_free = p.doi("scalar", CPY(kz32T, tp_view), sc)
                if gla_upto < 4:
                    continue
                p.w("vector", sc, g_tp_free)
                for ii in range(3):
                    p.do("vector", TS(kztb[:, ii * 128:(ii + 1) * 128], kz32T, bmask[:, ii:ii + 1],
                                      None, ALU.mult))
                v_kzb = p.doi("vector", TS(kztb[:, 384:512], kz32T, bmask[:, 3:4],
                                           None, ALU.mult), dv)
                if gla_upto < 5:
                    continue
                p.w("tensor", dv, v_kzb)
                p.w("tensor", sc, g_w_free)
                p.w("tensor", sc, g_vt_free)
                for ii in range(3):
                    p.do("tensor", MM(w_ps[:, ii * 128:(ii + 1) * 128],
                                      kztb[:, ii * 128:(ii + 1) * 128], vts_g[:, Bs], ii == 0, False))
                v_W = p.doi("tensor", MM(w_ps[:, 384:512], kztb[:, 384:512], vts_g[:, Bs],
                                         False, True), pe)
                p.w("scalar", pe, v_W)
                p.w("scalar", dv, g_chain_done)
                g_w_free = p.doi("scalar", CPY(wsb, w_ps[:, :]), sc)
                if gla_upto < 6:
                    continue
                p.w("tensor", dv, v_qz)
                p.w("tensor", sc, g_p_free)
                for h in range(3):
                    hs = slice(h * 128, (h + 1) * 128)
                    p.do("tensor", MM(p1_ps[:, hs], kzd, qz16[:, hs], h == 0, False))
                v_P1 = p.doi("tensor", MM(p1_ps[:, 384:512], kzd, qz16[:, 384:512], False, True), pe)
                p.w("scalar", pe, v_P1)
                v_p1c = p.doi("scalar", CPY(p1sb, p1_ps[:, :]), sc)
                p.w("tensor", sc, v_p1c)
                for h in range(3):
                    hs = slice(h * 128, (h + 1) * 128)
                    p.do("tensor", MM(p1_ps[:, hs], kzx, qz16[:, hs], h == 0, False))
                v_P2 = p.doi("tensor", MM(p1_ps[:, 384:512], kzx, qz16[:, 384:512], False, True), pe)
                p.w("scalar", pe, v_P2)
                g_p_free = p.doi("scalar", CPY(p2sb, p1_ps[:, :]), sc)
                p.w("vector", sc, g_p_free)
                for h in range(4):
                    hs = slice(h * 128, (h + 1) * 128)
                    p.do("vector", TT(pm1[:, hs], p1sb[:, hs], m16[:, :], ALU.mult))
                for h in range(3):
                    hs = slice(h * 128, (h + 1) * 128)
                    p.do("vector", TT(pm2[:, hs], p2sb[:, hs], mx16[:, :], ALU.mult))
                v_pm = p.doi("vector", TT(pm2[:, 384:512], p2sb[:, 384:512], mx16[:, :], ALU.mult), dv)
                if gla_upto < 7:
                    continue
                Scur = Ssb[:, pb * 128:(pb + 1) * 128]
                Sprev = Ssb[:, (1 - pb) * 128:(2 - pb) * 128]
                p.w("vector", sc, g_w_free)
                p.w("vector", sc, v_ls)
                p.w("vector", pe, g_o2mm_prev)
                chain_src = [Sprev, Rf[:, 0:128], Rf[:, 128:256], Rf[:, 256:384]]
                chain_dst = [Rf[:, 0:128], Rf[:, 128:256], Rf[:, 256:384], Scur]
                for ii in range(4):
                    vci = p.doi("vector", STT(chain_dst[ii], chain_src[ii], ls[:, ii:ii + 1],
                                              wsb[:, ii * 128:(ii + 1) * 128], ALU.mult, ALU.add), dv)
                g_chain_done = vci
                p.w("tensor", sc, g_o_free)
                p.w("tensor", dv, v_pm)
                for h in range(4):
                    hs = slice(h * 128, (h + 1) * 128)
                    p.do("tensor", MM(o_ps[:, hs], vts_g[:, Bs], pm1[:, hs], h == 0, False))
                for h in range(4):
                    hs = slice(h * 128, (h + 1) * 128)
                    p.do("tensor", MM(o_ps[:, hs], vts_g[:, Bs], pm2[:, hs], False, h == 3))
                p.w("tensor", sc, g_o2_free)
                p.w("tensor", dv, g_chain_done)
                o2lhs = [Sprev, Rf[:, 0:128], Rf[:, 128:256], Rf[:, 256:384]]
                for ii in range(4):
                    mmfn = MM(o2_ps[:, ii * 128:(ii + 1) * 128], o2lhs[ii],
                              qz32[:, ii * 128:(ii + 1) * 128], ii == 0, ii == 3)
                    if ii == 3:
                        v_o = p.doi("tensor", mmfn, pe)
                    else:
                        p.do("tensor", mmfn)
                g_o2mm_prev = v_o
                g_elem_free[pb] = v_o
                p.w("scalar", pe, v_o)
                g_o_free = p.doi("scalar", CPY(osb, o_ps[:, :]), sc)
                g_o2_free = p.doi("scalar", CPY(
                    o2sb.rearrange("p (h c) -> p h c", h=4),
                    o2_ps[:, :].rearrange("p (i h t) -> p h i t", i=4, h=4)), sc)
                p.w("vector", sc, g_o2_free)
                p.w("vector", ld, g_ostg_free[bufp])
                g_oevac = p.doi("vector", TT(
                    ostg[:, bufp * 2048:(bufp + 1) * 2048].rearrange("p (h c) -> p h c", h=4)[:, :, Bs],
                    osb.rearrange("p (h c) -> p h c", h=4),
                    o2sb.rearrange("p (h c) -> p h c", h=4), ALU.add), dv)
                g_elem_free_dv[pb] = g_oevac
            if gla_upto < 7:
                g_xn_free[bufp] = p.cur(pe)
                continue
            p.w("sync", dv, g_oevac)
            g_ostg_free[bufp] = p.doi("sync", DMA(
                cc_b[j * 512:(j + 1) * 512, col0:col0 + 512].rearrange("(h e) t -> e h t", e=128),
                ostg[:, bufp * 2048:(bufp + 1) * 2048].rearrange("p (h t) -> p h t", h=4)), ld, 16)
            g_xn_free[bufp] = p.cur(pe)
            g_vts_free[bufp] = p.cur(pe)
        p.w("gpsimd", ld, p.cur(ld))
        p.doi("gpsimd", lambda E: E.collective_compute(
            "AllGather", ALU.bypass, replica_groups=GROUPS,
            ins=[cc_b.ap()], outs=[gx_b.ap()]), cc, 1)
        esGP.close()
        esG.close()

    def emit_postres(src_dram, gather, tag, final=False):
        esR = ExitStack()
        Ar = _alloc(esR, nc, f"bufA_{tag}", [128, NK * TPC], BF16)
        Arv = [Ar[:, k * TPC:(k + 1) * TPC] for k in range(NK)]
        a_tr = _alloc(esR, nc, f"a_t_{tag}", [128, TPC], F32)
        bp_tr = _alloc(esR, nc, f"bp_t_{tag}", [128, TPC], F32)
        vtmpr = _alloc(esR, nc, f"vtmp_{tag}", [128, 512], F32)
        vtmp2r = _alloc(esR, nc, f"vtmp2_{tag}", [128, 512], F32)
        x2bufr = _alloc(esR, nc, f"x2buf_{tag}", [128, NK * 512], BF16)
        fout = _alloc(esR, nc, f"fout_{tag}", [128, 2 * TPC], F32) if final else None
        esRS = ExitStack()
        s1r = esRS.enter_context(nc.psum_tensor(f"s1_{tag}", [128, 512], F32))
        s2r = esRS.enter_context(nc.psum_tensor(f"s2_{tag}", [128, 512], F32))
        v_ld0 = p.cur(ld)
        v_lw0 = p.cur(lw)
        p.barrier(bar)
        for e in ENGINES:
            p.w(e, ld, v_ld0)
            p.w(e, lw, v_lw0)
        p.w("gpsimd", cc, p.cur(cc))
        from concourse.bass_core_rust import IndirectOffsetOnAxis
        for k in range(NK):
            if gather:
                p.doi("gpsimd", lambda E, k=k, src=src_dram: E.indirect_dma_start(
                    out=Arv[k], out_offset=None, in_=src.ap(),
                    in_offset=IndirectOffsetOnAxis(oidx_s[:, k:k + 1], 0)), lw, 16)
            else:
                p.doi("gpsimd", lambda E, k=k, src=src_dram: E.dma_start(
                    out=Arv[k], in_=src[k * 128:(k + 1) * 128, :]), lw, 16)
        v_gat = p.cur(lw)
        for e in ("vector", "scalar", "tensor"):
            p.w(e, lw, v_gat)
        ln_scalars(Arv, a_tr, bp_tr, vtmpr, vtmp2r, x2bufr, s1r, s2r)
        v_ln = p.cur(dv)
        p.w("vector", dv, v_ln)
        last = None
        for k in range(NK):
            for tt in range(4):
                ts = slice(tt * 512, (tt + 1) * 512)
                p.do("vector", TTg(Arv[k][:, ts], Arv[k][:, ts], a_tr[:, ts], ALU.mult))
                p.do("vector", TTg(Arv[k][:, ts], Arv[k][:, ts], bp_tr[:, ts], ALU.subtract))
                dst = fout[:, (k % 2) * TPC:(k % 2) * TPC + TPC][:, ts] if final else Xv[k][:, ts]
                fn = TTg(dst, Xv[k][:, ts], Arv[k][:, ts], ALU.add)
                if final:
                    if (k, tt) == (0, 0):
                        st_basef = p.cur(st)
                    p.do("vector", fn)
                    if tt == 3:
                        if k >= 2:
                            p.w("vector", st, st_basef + (k - 1) * 16)
                        v_f = p.doi("vector", lambda E: E.nop(), dv)
                else:
                    if (k, tt) == (NK - 1, 3):
                        last = p.doi("vector", fn, dv)
                    else:
                        p.do("vector", fn)
        if final:
            pass
        esRS.close()
        esR.close()
        return p.cur(dv)

    def TTg(out, a, b, op):
        return lambda E, o=out, a=a, b=b, op=op: E.tensor_tensor(o, a, b, op)

    if nstage >= 4:
        emit_postres(gx_b, True, "r0")

    def emit_ffn(layer, wg_d, wu_d, wd_d, gs_d, us_d, ds_d, h_d, ffo_d, tag):
        dequant = (layer == 1)
        esF0 = ExitStack()
        al0 = lambda n, sh, dt: esF0.enter_context(nc.sbuf_tensor(f"{n}_{tag}", sh, dt))
        gs_s = al0("gs_s", [128, NI], F32)
        us_s = al0("us_s", [128, NI], F32)
        ds_s = al0("ds_s", [128, NK], F32)
        srow = al0("srow", [1, 8 * 512], BF16)
        n128r = al0("n128r", [1, 128], BF16)
        esF = ExitStack()
        alf = lambda n, sh, dt: esF.enter_context(nc.sbuf_tensor(f"{n}_{tag}", sh, dt))
        XN = alf("xn", [128, NK * TPC], BF16)
        XNv = [XN[:, k * TPC:(k + 1) * TPC] for k in range(NK)]
        a_tf = alf("a_tf", [128, TPC], F32)
        bp_tf = alf("bp_tf", [128, TPC], F32)
        vtmpf = alf("vtmpf", [128, 512], F32)
        vtmp2f = alf("vtmp2f", [128, 512], F32)
        x2buff = alf("x2buff", [128, NK * 512], BF16)
        wgu = alf("wgu", [128, 2 * 2 * NK * 128], BF16)
        gu_sb = alf("gu_sb", [128, 2 * 2 * 512], BF16)
        h_sb = alf("h_sb", [128, 2 * TPC], BF16)
        esFS = ExitStack()
        psf = lambda n, sh: esFS.enter_context(nc.psum_tensor(f"{n}_{tag}", sh, F32))
        zg = psf("zg", [128, 2 * 512])
        zu = psf("zu", [128, 2 * 512])
        smh = psf("smh", [128, 4 * 512])
        dps = [zg[:, 0:512], zg[:, 512:1024], zu[:, 0:512], zu[:, 512:1024]]

        def wguv(par, gu, k):
            return wgu[:, ((par * 2 + gu) * NK + k) * 128:((par * 2 + gu) * NK + k + 1) * 128]

        v_ld0 = p.cur(ld)
        v_lw0 = p.cur(lw)
        p.barrier(bar)
        for e in ENGINES:
            p.w(e, ld, v_ld0)
            p.w(e, lw, v_lw0)
        p.doi("sync", DMA(gs_s[:, :], gs_d), ld, 16)
        p.doi("sync", DMA(us_s[:, :], us_d), ld, 16)
        v_scl = p.doi("sync", DMA(ds_s[:, :], ds_d), ld, 16)
        p.do("vector", lambda E: E.memset(n128r[:, :], -128.0))
        wready = [0] * NI
        for par, i in ((0, 0), (1, 1)):
            p.doi("gpsimd", DMA(
                wgu[:, (par * 2) * NK * 128:(par * 2 + 1) * NK * 128].rearrange("p (k q) -> p k q", k=NK),
                wg_d[:, i * 128:(i + 1) * 128].rearrange("(k p) q -> p k q", p=128)), lw, 16)
            wready[i] = p.doi("gpsimd", DMA(
                wgu[:, (par * 2 + 1) * NK * 128:(par * 2 + 2) * NK * 128].rearrange("p (k q) -> p k q", k=NK),
                wu_d[:, i * 128:(i + 1) * 128].rearrange("(k p) q -> p k q", p=128)), lw, 16)
        ln_scalars(Xv, a_tf, bp_tf, vtmpf, vtmp2f, x2buff, zg[:, 0:512], zg[:, 512:1024])
        v_lnf = p.cur(dv)
        p.w("vector", dv, v_lnf)
        for k in range(NK):
            for tt in range(4):
                ts = slice(tt * 512, (tt + 1) * 512)
                p.do("vector", TTg(XNv[k][:, ts], Xv[k][:, ts], a_tf[:, ts], ALU.mult))
                if (k, tt) != (NK - 1, 3):
                    p.do("vector", TTg(XNv[k][:, ts], XNv[k][:, ts], bp_tf[:, ts], ALU.subtract))
        _l = slice(3 * 512, 4 * 512)
        v_xnf = p.doi("vector", TTg(XNv[NK - 1][:, _l], XNv[NK - 1][:, _l], bp_tf[:, _l], ALU.subtract), dv)
        v_sxn = p.cur(sc)
        if dequant:
            p.w("tensor", dv, v_xnf)
            for tt in range(4):
                ts = slice(tt * 512, (tt + 1) * 512)
                for k in range(NK):
                    p.do("tensor", MM(smh[:, tt * 512:(tt + 1) * 512], ones128[:, :],
                                      XNv[k][:, ts], k == 0, k == NK - 1))
            v_mm = p.doi("tensor", lambda E: E.drain(), pe)
            p.w("scalar", pe, v_mm)
            for tt in range(3):
                p.do("scalar", CPY(srow[:, tt * 512:(tt + 1) * 512],
                                   smh[0:1, tt * 512:(tt + 1) * 512]))
            v_sxn = p.doi("scalar", CPY(srow[:, 3 * 512:4 * 512], smh[0:1, 3 * 512:4 * 512]), sc)
        p.w("tensor", ld, v_scl)
        p.w("tensor", dv, v_xnf)
        p.w("scalar", ld, v_scl)
        g_zfree = [max(p.cur(sc), v_sxn)] * 2
        g_gufree = [0, 0]
        g_wfree = [0, 0]
        g_hfree = [0, 0]
        v_hrow = [0, 0, 0, 0]
        for i in range(NI):
            par = i % 2
            p.w("tensor", lw, wready[i])
            for tt in range(4):
                ts = slice(tt * 512, (tt + 1) * 512)
                zpar = (i * 4 + tt) % 2
                zgv = zg[:, zpar * 512:(zpar + 1) * 512]
                zuv = zu[:, zpar * 512:(zpar + 1) * 512]
                p.w("tensor", sc, g_zfree[zpar])
                for k in range(NK - 1):
                    p.do("tensor", MM(zgv, wguv(par, 0, k), XNv[k][:, ts], k == 0, False))
                if dequant:
                    p.do("tensor", MM(zgv, wguv(par, 0, NK - 1), XNv[NK - 1][:, ts], False, False))
                    p.w("tensor", sc, v_sxn)
                    p.do("tensor", MM(zgv, n128r[:, :], srow[:, tt * 512:(tt + 1) * 512], False, True))
                else:
                    p.do("tensor", MM(zgv, wguv(par, 0, NK - 1), XNv[NK - 1][:, ts], False, True))
                for k in range(NK - 1):
                    p.do("tensor", MM(zuv, wguv(par, 1, k), XNv[k][:, ts], k == 0, False))
                if dequant:
                    p.do("tensor", MM(zuv, wguv(par, 1, NK - 1), XNv[NK - 1][:, ts], False, False))
                    v_gu = p.doi("tensor", MM(zuv, n128r[:, :], srow[:, tt * 512:(tt + 1) * 512], False, True), pe)
                else:
                    v_gu = p.doi("tensor", MM(zuv, wguv(par, 1, NK - 1), XNv[NK - 1][:, ts], False, True), pe)
                gv = gu_sb[:, (zpar * 2) * 512:(zpar * 2 + 1) * 512]
                uv = gu_sb[:, (zpar * 2 + 1) * 512:(zpar * 2 + 2) * 512]
                p.w("scalar", pe, v_gu)
                p.w("scalar", dv, g_gufree[zpar])
                p.do("scalar", lambda E, o=gv, a=zgv, i=i: E.activation(
                    o, a, AF.Silu, bias=0.0, scale=gs_s[:, i:i + 1]))
                vsc = p.doi("scalar", lambda E, o=uv, a=zuv, i=i: E.activation(
                    o, a, AF.Copy, bias=0.0, scale=us_s[:, i:i + 1]), sc)
                g_zfree[zpar] = vsc
                p.w("vector", sc, vsc)
                p.w("vector", ld, g_hfree[par])
                vdv = p.doi("vector", TTg(h_sb[:, par * TPC + tt * 512:par * TPC + (tt + 1) * 512],
                                          gv, uv, ALU.mult), dv)
                g_gufree[zpar] = vdv
                if dequant:
                    p.w("tensor", dv, vdv)
                    if i == 0:
                        p.w("tensor", sc, v_sxn)
                    v_hrow[tt] = p.doi("tensor", MM(
                        smh[:, tt * 512:(tt + 1) * 512], ones128[:, :],
                        h_sb[:, par * TPC + tt * 512:par * TPC + (tt + 1) * 512],
                        i == 0, i == NI - 1), pe)
            g_wfree[par] = p.cur(pe)
            if i + 2 < NI:
                p.w("gpsimd", pe, g_wfree[par])
                p.doi("gpsimd", DMA(
                    wgu[:, (par * 2) * NK * 128:(par * 2 + 1) * NK * 128].rearrange("p (k q) -> p k q", k=NK),
                    wg_d[:, (i + 2) * 128:(i + 3) * 128].rearrange("(k p) q -> p k q", p=128)), lw, 16)
                wready[i + 2] = p.doi("gpsimd", DMA(
                    wgu[:, (par * 2 + 1) * NK * 128:(par * 2 + 2) * NK * 128].rearrange("p (k q) -> p k q", k=NK),
                    wu_d[:, (i + 2) * 128:(i + 3) * 128].rearrange("(k p) q -> p k q", p=128)), lw, 16)
            p.w("sync", dv, p.cur(dv))
            g_hfree[par] = p.doi("sync", DMA(
                h_d[i * 128:(i + 1) * 128, :], h_sb[:, par * TPC:(par + 1) * TPC]), ld, 16)
        v_smh = p.cur(sc)
        if dequant:
            for tt in range(4):
                p.w("scalar", pe, v_hrow[tt])
                if tt < 3:
                    p.do("scalar", CPY(srow[:, (4 + tt) * 512:(5 + tt) * 512],
                                       smh[0:1, tt * 512:(tt + 1) * 512]))
            v_smh = p.doi("scalar", CPY(srow[:, 7 * 512:8 * 512], smh[0:1, 3 * 512:4 * 512]), sc)
        v_hlast = p.cur(ld)
        v_pe0 = p.cur(pe)
        p.barrier(bar)
        for e in ENGINES:
            p.w(e, ld, v_hlast)
            p.w(e, lw, p.cur(lw))
        esFS.close()
        esF.close()
        esD = ExitStack()
        wd_s = esD.enter_context(nc.sbuf_tensor(f"wd_{tag}", [128, NI * 4 * 128], BF16))
        h_t = esD.enter_context(nc.sbuf_tensor(f"ht_{tag}", [128, 2 * 22 * 512], BF16))
        fo_sb = esD.enter_context(nc.sbuf_tensor(f"fo_{tag}", [128, 4 * TPC], BF16))
        esDS = ExitStack()
        dn_ps = esDS.enter_context(nc.psum_tensor(f"dn_{tag}", [128, 4 * 512], F32))
        dpsv = [dn_ps[:, d * 512:(d + 1) * 512] for d in range(4)]
        g_htfree = [0, 0]
        g_dfree = [0, 0, 0, 0]
        wdA = wd_s[:, 0:22 * 512]
        wdB = wd_s[:, 22 * 512:NI * 512]
        g_wdAfree = 0
        g_wdBfree = 0
        v_wdA = p.doi("gpsimd", DMA(
            wdA.rearrange("p (i q) -> p i q", i=22),
            wd_d[0:22 * 128, 0:512].rearrange("(i p) q -> p i q", p=128)), lw, 16)
        v_wdB = p.doi("gpsimd", DMA(
            wdB.rearrange("p (i q) -> p i q", i=22),
            wd_d[22 * 128:NI * 128, 0:512].rearrange("(i p) q -> p i q", p=128)), lw, 16)
        for pp in range(4):
            if pp > 0:
                p.w("gpsimd", pe, g_wdBfree)
                v_wdB = p.doi("gpsimd", DMA(
                    wdB.rearrange("p (i q) -> p i q", i=22),
                    wd_d[22 * 128:NI * 128, pp * 512:(pp + 1) * 512].rearrange("(i p) q -> p i q", p=128)),
                    lw, 16)
            p.w("tensor", lw, v_wdA)
            for tt in range(4):
                ts = slice(tt * 512, (tt + 1) * 512)
                vlast = [0, 0, 0, 0]
                for hh in range(2):
                    if hh == 1 and tt == 0:
                        p.w("tensor", lw, v_wdB)
                    p.w("sync", pe, g_htfree[hh])
                    v_ht = p.doi("sync", DMA(
                        h_t[:, hh * 22 * 512:(hh + 1) * 22 * 512].rearrange("p (i q) -> p i q", i=22),
                        h_d[hh * 22 * 128:(hh + 1) * 22 * 128, ts].rearrange("(i p) q -> p i q", p=128)), ld, 16)
                    p.w("tensor", ld, v_ht)
                    for d in range(4):
                        if hh == 0:
                            p.w("tensor", sc, g_dfree[d])
                        for i in range(22):
                            ii = hh * 22 + i
                            first = (hh == 0 and i == 0)
                            last = (hh == 1 and i == 21)
                            wtile = wd_s[:, (ii * 4 + d) * 128:(ii * 4 + d + 1) * 128]
                            htile = h_t[:, (hh * 22 + i) * 512:(hh * 22 + i + 1) * 512]
                            if last:
                                if dequant:
                                    p.do("tensor", MM(dpsv[d], wtile, htile, False, False))
                                    p.w("tensor", sc, v_smh)
                                    vlast[d] = p.doi("tensor", MM(dpsv[d], n128r[:, :],
                                                                  srow[:, (4 + tt) * 512:(5 + tt) * 512],
                                                                  False, True), pe)
                                else:
                                    vlast[d] = p.doi("tensor", MM(dpsv[d], wtile, htile, False, True), pe)
                            elif hh == 0 and i == 21:
                                p.doi("tensor", MM(dpsv[d], wtile, htile, first, False), pe)
                            else:
                                p.do("tensor", MM(dpsv[d], wtile, htile, first, False))
                    g_htfree[hh] = p.cur(pe)
                    if tt == 3 and hh == 0:
                        g_wdAfree = p.cur(pe)
                for d in range(4):
                    dout = pp * 4 + d
                    p.w("scalar", pe, vlast[d])
                    g_dfree[d] = p.doi("scalar", lambda E, d=d, dout=dout, ts=ts: E.activation(
                        fo_sb[:, d * TPC:(d + 1) * TPC][:, ts],
                        dpsv[d], AF.Copy, bias=0.0, scale=ds_s[:, dout:dout + 1]), sc)
            g_wdBfree = p.cur(pe)
            if pp + 1 < 4:
                p.w("gpsimd", pe, g_wdAfree)
                v_wdA = p.doi("gpsimd", DMA(
                    wdA.rearrange("p (i q) -> p i q", i=22),
                    wd_d[0:22 * 128, (pp + 1) * 512:(pp + 2) * 512].rearrange("(i p) q -> p i q", p=128)),
                    lw, 16)
            p.w("gpsimd", sc, p.cur(sc))
            p.doi("gpsimd", DMA(
                ffo_d[pp * 512:(pp + 1) * 512, :].rearrange("(d p) t -> p d t", p=128),
                fo_sb[:, :].rearrange("p (d t) -> p d t", d=4)), ld, 16)
        p.w("gpsimd", ld, p.cur(ld))
        esDS.close()
        esD.close()
        esF0.close()

    if nstage >= 5:
        emit_ffn(0, gw_t, uw_t, dw_t, gs_r, us_r, ds_r, h_dram, ffo_dram, "f0")
    if nstage >= 6:
        emit_postres(ffo_dram, False, "r1")

    if nstage >= 7:
        esP7 = ExitStack()
        A7 = _alloc(esP7, nc, "bufA7", [128, NK * TPC], BF16)
        A7v = [A7[:, k * TPC:(k + 1) * TPC] for k in range(NK)]
        a_t7 = _alloc(esP7, nc, "a_t7", [128, TPC], F32)
        bp_t7 = _alloc(esP7, nc, "bp_t7", [128, TPC], F32)
        vtmp7 = _alloc(esP7, nc, "vtmp7", [128, 512], F32)
        vtmp27 = _alloc(esP7, nc, "vtmp27", [128, 512], F32)
        x2buf7 = _alloc(esP7, nc, "x2buf7", [128, NK * 512], BF16)
        esPS7 = ExitStack()
        s17 = esPS7.enter_context(nc.psum_tensor("s17", [128, 512], F32))
        s27 = esPS7.enter_context(nc.psum_tensor("s27", [128, 512], F32))
        v_ld0 = p.cur(ld)
        v_lw0 = p.cur(lw)
        p.barrier(bar)
        for e in ENGINES:
            p.w(e, ld, v_ld0)
            p.w(e, lw, v_lw0)
        ln_scalars(Xv, a_t7, bp_t7, vtmp7, vtmp27, x2buf7, s17, s27)
        v_ln7 = p.cur(dv)
        p.w("vector", dv, v_ln7)
        for k in range(NK):
            for tt in range(4):
                ts = slice(tt * 512, (tt + 1) * 512)
                p.do("vector", TTg(A7v[k][:, ts], Xv[k][:, ts], a_t7[:, ts], ALU.mult))
                if (k, tt) != (NK - 1, 3):
                    p.do("vector", TTg(A7v[k][:, ts], A7v[k][:, ts], bp_t7[:, ts], ALU.subtract))
        _l7 = slice(3 * 512, 4 * 512)
        v_xn7 = p.doi("vector", TTg(A7v[NK - 1][:, _l7], A7v[NK - 1][:, _l7], bp_t7[:, _l7], ALU.subtract), dv)
        p.w("sync", dv, v_xn7)
        p.doi("sync", lambda E: E.dma_start(
            out=cc_c.ap().rearrange("(k p) t -> p k t", p=128),
            in_=A7[:, :].rearrange("p (k t) -> p k t", k=NK)), ld, 16)
        v_spill = p.doi("sync", lambda E: E.dma_start(
            out=xspill.ap().rearrange("(k p) t -> p k t", p=128),
            in_=X[:, :].rearrange("p (k t) -> p k t", k=NK)), ld, 16)
        p.w("gpsimd", ld, p.cur(ld))
        p.doi("gpsimd", lambda E: E.collective_compute(
            "AllGather", ALU.bypass, replica_groups=GROUPS,
            ins=[cc_c.ap()], outs=[gx_c.ap()]), cc, 1)
        v_cc7 = p.cur(cc)
        esPS7.close()
        esP7.close()

        esA = ExitStack()
        alA = lambda n, sh, dt: esA.enter_context(nc.sbuf_tensor(n, sh, dt))
        Kt = X
        Vt = alA("Vt", [128, 4 * 64 * 128], BF16)
        esA1 = ExitStack()
        wkv = esA1.enter_context(nc.sbuf_tensor("wkv", [128, 4 * NK * 128], BF16))
        xnt7 = esA1.enter_context(nc.sbuf_tensor("xnt7", [128, 2 * NK * 512], BF16))
        csb = esA1.enter_context(nc.sbuf_tensor("csb", [128, 2 * 512], F32))
        ssb = esA1.enter_context(nc.sbuf_tensor("ssb", [128, 2 * 512], F32))
        rot_s = esA1.enter_context(nc.sbuf_tensor("rot_s", [128, 128], BF16))
        rtmp = esA1.enter_context(nc.sbuf_tensor("rtmp", [128, 4 * 512], F32))
        krw = esA1.enter_context(nc.sbuf_tensor("krw", [128, 4 * 512], BF16))
        esAS = ExitStack()
        kv_ps = esAS.enter_context(nc.psum_tensor("kv_ps", [128, 2 * 512], F32))
        rp_ps = esAS.enter_context(nc.psum_tensor("rp_ps", [128, 512], F32))

        p.doi("sync", DMA(rot_s[:, :], rot_t), ld, 16)
        p.barrier(bar)
        for e in ENGINES:
            p.w(e, ld, v_spill)
        p.w("gpsimd", cc, v_cc7)
        p.w("sync", cc, v_cc7)

        def kv_pass(which):
            g_xf = [p.cur(pe), p.cur(pe)]
            g_kvfree = [p.cur(sc), p.cur(sc)]
            g_csfree = [0, 0]
            g_krw_dv = [0, 0]
            base = 1024 if which else 512
            p.w("gpsimd", pe, p.cur(pe))
            for h in range(4):
                p.doi("gpsimd", DMA(
                    wkv[:, h * NK * 128:(h + 1) * NK * 128].rearrange("p (k q) -> p k q", k=NK),
                    wqkv_sl[:, base + h * 128:base + (h + 1) * 128].rearrange("(k p) q -> p k q", p=128)), lw, 16)
            v_wh = p.cur(lw)
            p.w("tensor", lw, v_wh)
            for tt in range(16):
                par = tt % 2
                j, c0 = tt // 4, (tt % 4) * 512
                p.w("sync", pe, g_xf[par])
                v_xnld = p.doi("sync", DMA(
                    xnt7[:, par * NK * 512:(par + 1) * NK * 512].rearrange("p (k t) -> p k t", k=NK),
                    gx_c[j * DM:(j + 1) * DM, c0:c0 + 512].rearrange("(k p) t -> p k t", p=128)), ld, 16)
                if which == 1:
                    p.w("sync", dv, g_csfree[par])
                    p.doi("sync", DMA(csb[:, par * 512:(par + 1) * 512],
                                      cos_d[:, tt * 512:(tt + 1) * 512]), ld, 16)
                    v_csld = p.doi("sync", DMA(ssb[:, par * 512:(par + 1) * 512],
                                               sin_d[:, tt * 512:(tt + 1) * 512]), ld, 16)
                xv7 = [xnt7[:, (par * NK + k) * 512:(par * NK + k + 1) * 512] for k in range(NK)]
                p.w("tensor", ld, v_xnld if which == 0 else p.cur(ld))
                for h in range(4):
                    kpar = (tt * 4 + h) % 2
                    kvv = kv_ps[:, kpar * 512:(kpar + 1) * 512]
                    wv = [wkv[:, (h * NK + k) * 128:(h * NK + k + 1) * 128] for k in range(NK)]
                    p.w("tensor", sc, g_kvfree[kpar])
                    if which == 0:
                        for tc in range(4):
                            tsl = slice(tc * 128, (tc + 1) * 128)
                            for k in range(NK):
                                lastk = (tc == 3 and k == NK - 1)
                                mfn = MM(kvv[:, tsl], xv7[k][:, tsl], wv[k],
                                         tc == 0 and k == 0, lastk)
                                if lastk:
                                    v_kv = p.doi("tensor", mfn, pe)
                                else:
                                    p.do("tensor", mfn)
                        p.w("scalar", pe, v_kv)
                        g_kvfree[kpar] = p.doi("scalar", lambda E, h=h, tt=tt, kvv=kvv: E.copy(
                            Vt[:, :].rearrange("p (h c e) -> p h c e", h=4, c=64)[:, h, tt * 4:tt * 4 + 4, :],
                            kvv.rearrange("p (c e) -> p c e", c=4)), sc)
                    else:
                        rpar = (tt * 4 + h) % 2
                        kr0 = krw[:, rpar * 1024:rpar * 1024 + 512]
                        kr1 = krw[:, rpar * 1024 + 512:rpar * 1024 + 1024]
                        rt0 = rtmp[:, rpar * 1024:rpar * 1024 + 512]
                        rt1 = rtmp[:, rpar * 1024 + 512:rpar * 1024 + 1024]
                        for k in range(NK):
                            mfn = MM(kvv, wv[k], xv7[k], k == 0, k == NK - 1)
                            if k == NK - 1:
                                v_kv = p.doi("tensor", mfn, pe)
                            else:
                                p.do("tensor", mfn)
                        p.w("scalar", pe, v_kv)
                        p.w("scalar", dv, g_krw_dv[rpar])
                        v_raw = p.doi("scalar", CPY(kr0, kvv), sc)
                        p.w("tensor", sc, v_raw)
                        v_rot = p.doi("tensor", MM(rp_ps[:, :], rot_s[:, :], kr0, True, True), pe)
                        p.w("scalar", pe, v_rot)
                        v_rr = p.doi("scalar", CPY(kr1, rp_ps[:, :]), sc)
                        g_kvfree[kpar] = v_rr
                        p.w("vector", sc, v_rr)
                        p.w("vector", ld, v_csld)
                        p.do("vector", TTg(rt0, kr0, csb[:, par * 512:(par + 1) * 512], ALU.mult))
                        p.do("vector", TTg(rt1, kr1, ssb[:, par * 512:(par + 1) * 512], ALU.mult))
                        vk2 = p.doi("vector", TTg(Kt[:, (h * 16 + tt) * 512:(h * 16 + tt + 1) * 512],
                                                  rt0, rt1, ALU.add), dv)
                        g_krw_dv[rpar] = vk2
                        if h == 3:
                            g_csfree[par] = vk2
                g_xf[par] = p.cur(pe)
            return

        kv_pass(0)
        kv_pass(1)
        v_ld0 = p.cur(ld)
        v_lw0 = p.cur(lw)
        p.barrier(bar)
        for e in ENGINES:
            p.w(e, ld, v_ld0)
            p.w(e, lw, v_lw0)
        esAS.close()
        esA1.close()

        esA2 = ExitStack()
        alF = lambda n, sh, dt: esA2.enter_context(nc.sbuf_tensor(n, sh, dt))
        wq7 = alF("wq7", [128, 4 * NK * 128], BF16)
        xnq = alF("xnq", [128, NK * 512], BF16)
        cqb = alF("cqb", [128, 512], F32)
        sqb = alF("sqb", [128, 512], F32)
        rot_q = alF("rot_q", [128, 128], BF16)
        rtq = alF("rtq", [128, 2 * 512], F32)
        qrw = alF("qrw", [128, 2 * 512], BF16)
        qzt = alF("qzt", [128, 4 * 512], BF16)
        l1m_s = alF("l1m_s", [128, 4 * 512], BF16)
        pt_sb = alF("pt_sb", [128, 2 * 512], BF16)
        ebias = alF("ebias", [128, 1], F32)
        ob_sb = alF("ob_sb", [128, 512], BF16)
        rn_sb = alF("rn_sb", [128, 2 * 512], F32)
        ostg7 = alF("ostg7", [128, 2 * 4 * 512], BF16)
        esFS7 = ExitStack()
        q_ps = esFS7.enter_context(nc.psum_tensor("q_ps", [128, 512], F32))
        rq_ps = esFS7.enter_context(nc.psum_tensor("rq_ps", [128, 512], F32))
        s_ps = esFS7.enter_context(nc.psum_tensor("s_ps", [128, 2 * 512], F32))
        o7_ps2 = esFS7.enter_context(nc.psum_tensor("o7_ps", [128, 2 * 512], F32))
        n7_ps2 = esFS7.enter_context(nc.psum_tensor("n7_ps", [128, 2 * 512], F32))
        p.doi("sync", DMA(rot_q[:, :], rot_t), ld, 16)
        v_m7 = p.doi("sync", DMA(l1m_s[:, :], l1mask), ld, 16)
        p.w("tensor", ld, v_m7)
        p.w("vector", ld, v_m7)
        v_eb = p.doi("vector", lambda E: E.memset(ebias[:, :], EXP_SHIFT), dv)
        p.w("scalar", dv, v_eb)
        g_xq = [0, 0]
        g_cqfree = [0, 0]
        g_qps_free = 0
        g_rq_free = 0
        g_qz_free = 0
        g_sps_free = [0, 0]
        g_pt_free = [0, 0]
        g_pt_pe = [0, 0]
        g_o7_free = [0, 0]
        g_n7_free = [0, 0]
        g_ob_free = 0
        g_rn_free = 0
        g_ostg7_free = [0, 0]
        for h in range(4):
            v_wq7 = p.doi("gpsimd", DMA(
                wq7[:, h * NK * 128:(h + 1) * NK * 128].rearrange("p (k q) -> p k q", k=NK),
                wqkv_sl[:, h * 128:(h + 1) * 128].rearrange("(k p) q -> p k q", p=128)), lw, 16)
        for qt in range(16):
            par = qt % 2
            j, c0 = qt // 4, (qt % 4) * 512
            p.w("sync", pe, g_xq[0])
            v_xq = p.doi("sync", DMA(
                xnq[:, :].rearrange("p (k t) -> p k t", k=NK),
                gx_c[j * DM:(j + 1) * DM, c0:c0 + 512].rearrange("(k p) t -> p k t", p=128)), ld, 16)
            p.w("sync", dv, g_cqfree[0])
            p.doi("sync", DMA(cqb[:, :], cos_d[:, qt * 512:(qt + 1) * 512]), ld, 16)
            v_cq = p.doi("sync", DMA(sqb[:, :], sin_d[:, qt * 512:(qt + 1) * 512]), ld, 16)
            xvq = [xnq[:, k * 512:(k + 1) * 512] for k in range(NK)]
            nkc = 4 * qt + 4
            v_qz7h = [0, 0, 0, 0]
            for h in range(4):
                p.w("tensor", ld, v_xq)
                p.w("tensor", sc, g_qps_free)
                p.w("tensor", lw, v_wq7)
                for k in range(NK):
                    mfn = MM(q_ps[:, :], wq7[:, (h * NK + k) * 128:(h * NK + k + 1) * 128],
                             xvq[k], k == 0, k == NK - 1)
                    if k == NK - 1:
                        v_q = p.doi("tensor", mfn, pe)
                    else:
                        p.do("tensor", mfn)
                p.w("scalar", pe, v_q)
                p.w("scalar", dv, g_qz_free)
                g_qps_free = p.doi("scalar", CPY(qrw[:, 0:512], q_ps[:, :]), sc)
                p.w("tensor", sc, g_qps_free)
                p.w("tensor", sc, g_rq_free)
                v_rotq = p.doi("tensor", MM(rq_ps[:, :], rot_q[:, :], qrw[:, 0:512], True, True), pe)
                p.w("scalar", pe, v_rotq)
                g_rq_free = p.doi("scalar", CPY(qrw[:, 512:1024], rq_ps[:, :]), sc)
                p.w("vector", sc, g_rq_free)
                p.w("vector", ld, v_cq)
                p.do("vector", TTg(rtq[:, 0:512], qrw[:, 0:512], cqb[:, :], ALU.mult))
                p.do("vector", TTg(rtq[:, 512:1024], qrw[:, 512:1024], sqb[:, :], ALU.mult))
                v_qz7h[h] = p.doi("vector", TTg(qzt[:, h * 512:(h + 1) * 512],
                                                rtq[:, 0:512], rtq[:, 512:1024], ALU.add), dv)
                g_qz_free = v_qz7h[h]
            g_cqfree[0] = v_qz7h[3]
            for h in range(4):
                hpar = h % 2
                o7_ps = o7_ps2[:, hpar * 512:(hpar + 1) * 512]
                n7_ps = n7_ps2[:, hpar * 512:(hpar + 1) * 512]
                p.w("tensor", dv, v_qz7h[h])

                def emit_score(kc):
                    spar = kc % 2
                    sv = s_ps[:, spar * 512:(spar + 1) * 512]
                    p.w("tensor", sc, g_sps_free[spar])
                    ktile = Kt[:, :].rearrange("p (h q) -> p h q", h=4)[:, h, kc * 128:(kc + 1) * 128]
                    return p.doi("tensor", MM(sv, ktile, qzt[:, h * 512:(h + 1) * 512], True, True), pe)

                v_s = {0: emit_score(0)}
                for kc in range(nkc):
                    spar = kc % 2
                    sv = s_ps[:, spar * 512:(spar + 1) * 512]
                    p.w("scalar", pe, v_s[kc])
                    p.w("scalar", dv, g_pt_free[spar])
                    p.w("scalar", pe, g_pt_pe[spar])
                    vexp = p.doi("scalar", lambda E, sv=sv, spar=spar: E.activation(
                        pt_sb[:, spar * 512:(spar + 1) * 512], sv, AF.Exp,
                        bias=ebias[:, 0:1], scale=QSCALE), sc)
                    g_sps_free[spar] = vexp
                    diag = kc - 4 * qt
                    if diag >= 0:
                        p.w("vector", sc, vexp)
                        vexp_d = p.doi("vector", TTg(pt_sb[:, spar * 512:(spar + 1) * 512],
                                                     pt_sb[:, spar * 512:(spar + 1) * 512],
                                                     l1m_s[:, diag * 512:(diag + 1) * 512], ALU.mult), dv)
                        g_pt_free[spar] = vexp_d
                    if kc + 1 < nkc:
                        v_s[kc + 1] = emit_score(kc + 1)
                    if diag >= 0:
                        p.w("tensor", dv, vexp_d)
                    else:
                        p.w("tensor", sc, vexp)
                    vtile = Vt[:, :].rearrange("p (h c e) -> p h c e", h=4, c=64)[:, h, kc, :]
                    if kc == 0:
                        p.w("tensor", sc, g_o7_free[hpar])
                        p.w("tensor", sc, g_n7_free[hpar])
                    p.do("tensor", MM(o7_ps, vtile, pt_sb[:, spar * 512:(spar + 1) * 512],
                                      kc == 0, kc == nkc - 1))
                    v_on = p.doi("tensor", MM(n7_ps, ones128[:, :],
                                              pt_sb[:, spar * 512:(spar + 1) * 512],
                                              kc == 0, kc == nkc - 1), pe)
                    g_pt_pe[spar] = v_on
                p.w("scalar", pe, v_on)
                p.w("scalar", dv, g_rn_free)
                p.do("scalar", lambda E, n7_ps=n7_ps: E.activation(rn_sb[:, 0:512], n7_ps, AF.Ln))
                v_rn = p.doi("scalar", lambda E: E.activation(rn_sb[:, 512:1024], rn_sb[:, 0:512],
                                                              AF.Exp, bias=0.0, scale=-1.0), sc)
                g_n7_free[hpar] = v_rn
                p.w("scalar", dv, g_ob_free)
                v_ob = p.doi("scalar", lambda E, o7_ps=o7_ps: E.copy(ob_sb[:, :], o7_ps), sc)
                g_o7_free[hpar] = v_ob
                p.w("vector", sc, v_ob)
                p.w("vector", ld, g_ostg7_free[par])
                v_of = p.doi("vector", TTg(ostg7[:, (par * 4 + h) * 512:(par * 4 + h + 1) * 512],
                                           ob_sb[:, :], rn_sb[:, 512:1024], ALU.mult), dv)
                g_ob_free = v_of
                g_rn_free = v_of
            p.w("sync", dv, p.cur(dv))
            g_ostg7_free[par] = p.doi("sync", DMA(
                cc_d[j * 512:(j + 1) * 512, c0:c0 + 512].rearrange("(h e) t -> e h t", e=128),
                ostg7[:, par * 2048:(par + 1) * 2048].rearrange("p (h t) -> p h t", h=4)), ld, 16)
            g_xq[0] = p.cur(pe)
        p.w("gpsimd", ld, p.cur(ld))
        p.doi("gpsimd", lambda E: E.collective_compute(
            "AllGather", ALU.bypass, replica_groups=GROUPS,
            ins=[cc_d.ap()], outs=[gx_d.ap()]), cc, 1)
        esFS7.close()
        esA2.close()
        esA.close()

        v_ld0 = p.cur(ld)
        v_lw0 = p.cur(lw)
        p.barrier(bar)
        for e in ENGINES:
            p.w(e, ld, v_ld0)
            p.w(e, lw, v_lw0)
        p.doi("gpsimd", lambda E: E.dma_start(
            out=X[:, :].rearrange("p (k t) -> p k t", k=NK),
            in_=xspill.ap().rearrange("(k p) t -> p k t", p=128)), lw, 16)
        esO = ExitStack()
        OF = esO.enter_context(nc.sbuf_tensor("OF", [128, NK * TPC], BF16))
        OFv = [OF[:, k * TPC:(k + 1) * TPC] for k in range(NK)]
        wo_s = esO.enter_context(nc.sbuf_tensor("wo_s", [128, 2 * NK * 128], BF16))
        ao_sb = esO.enter_context(nc.sbuf_tensor("ao_sb", [128, 2 * TPC], BF16))
        esOS = ExitStack()
        op_ps = esOS.enter_context(nc.psum_tensor("op_ps", [128, 2 * 512], F32))
        from concourse.bass_core_rust import IndirectOffsetOnAxis
        p.w("gpsimd", cc, p.cur(cc))
        gatld = {}
        for k in range(NK):
            gatld[k] = p.doi("gpsimd", lambda E, k=k: E.indirect_dma_start(
                out=OFv[k], out_offset=None, in_=gx_d.ap(),
                in_offset=IndirectOffsetOnAxis(oidx_s[:, k:k + 1], 0)), lw, 16)
        g_wofree = [0, 0]
        g_aofree = [0, 0]
        g_opfree = [0, 0]
        woldv = {}
        for dk in (0, 1):
            woldv[dk] = p.doi("gpsimd", DMA(
                wo_s[:, (dk % 2) * NK * 128:((dk % 2) + 1) * NK * 128].rearrange("p (k q) -> p k q", k=NK),
                wout_t[:, dk * 128:(dk + 1) * 128].rearrange("(k p) q -> p k q", p=128)), lw, 16)
        for dk in range(NK):
            wpar = dk % 2
            p.w("tensor", lw, woldv[dk])
            p.w("tensor", lw, gatld[NK - 1])
            for tt in range(4):
                ts = slice(tt * 512, (tt + 1) * 512)
                opar = (dk * 4 + tt) % 2
                opv = op_ps[:, opar * 512:(opar + 1) * 512]
                p.w("tensor", sc, g_opfree[opar])
                for k in range(NK):
                    mfn = MM(opv, wo_s[:, (wpar * NK + k) * 128:(wpar * NK + k + 1) * 128],
                             OFv[k][:, ts], k == 0, k == NK - 1)
                    if k == NK - 1:
                        v_op = p.doi("tensor", mfn, pe)
                    else:
                        p.do("tensor", mfn)
                p.w("scalar", pe, v_op)
                p.w("scalar", ld, g_aofree[dk % 2])
                g_opfree[opar] = p.doi("scalar", CPY(
                    ao_sb[:, (dk % 2) * TPC:(dk % 2) * TPC + TPC][:, ts], opv), sc)
            if dk + 2 < NK:
                p.w("gpsimd", pe, p.cur(pe))
                woldv[dk + 2] = p.doi("gpsimd", DMA(
                    wo_s[:, wpar * NK * 128:(wpar + 1) * NK * 128].rearrange("p (k q) -> p k q", k=NK),
                    wout_t[:, (dk + 2) * 128:(dk + 3) * 128].rearrange("(k p) q -> p k q", p=128)), lw, 16)
            p.w("sync", sc, p.cur(sc))
            g_aofree[dk % 2] = p.doi("sync", DMA(
                a2o_dram[dk * 128:(dk + 1) * 128, :],
                ao_sb[:, (dk % 2) * TPC:(dk % 2) * TPC + TPC]), ld, 16)
        p.w("gpsimd", ld, p.cur(ld))
        esOS.close()
        esO.close()

    if nstage >= 8:
        emit_postres(a2o_dram, False, "r2")
    if nstage >= 9:
        emit_ffn(1, qg_t, qu_t, qd_t, sg1_r, su1_r, sd1_r, h2_dram, ffo2_dram, "f1")
    if nstage >= 10:
        emit_postres(ffo2_dram, False, "r3")

    xout = sb("xout", [128, 2 * TPC], F32)
    dbg16 = sb("dbg16", [128, 2 * TPC], BF16)

    def dump16(views, dest):
        st_base = p.cur(st)
        for k in range(NK):
            buf = xout[:, (k % 2) * TPC:(k % 2 + 1) * TPC]
            p.w("scalar", st, st_base + (k - 1) * 16)
            v_up = p.doi("scalar", lambda E, k=k, buf=buf: E.activation(buf, views[k], AF.Copy), sc)
            p.w("sync", sc, v_up)
            p.doi("sync", lambda E, k=k, buf=buf, dest=dest: E.dma_start(
                out=dest[k * 128:(k + 1) * 128, :], in_=buf), st, 16)

    p.w("scalar", lw, 16)
    p.w("scalar", dv, p.cur(dv))
    p.w("scalar", cc, p.cur(cc))
    p.w("gpsimd", ld, p.cur(ld))
    p.w("gpsimd", cc, p.cur(cc))
    dump16(Xv, o_sl)
    if dbg_what == "xn":
        dump16(Av, dbg)
    if dbg_what in ("ccb", "gxb"):
        from concourse.bass_core_rust import IndirectOffsetOnAxis
        st_base = p.cur(st)
        v_ups = {}
        for k in range(NK):
            buf16 = dbg16[:, (k % 2) * 2048:(k % 2 + 1) * 2048]
            fbuf = xout[:, (k % 2) * TPC:(k % 2 + 1) * TPC]
            if k >= 2:
                p.w("gpsimd", sc, v_ups[k - 2])
            if dbg_what == "ccb":
                v_l = p.doi("gpsimd", lambda E, b=buf16, k=k: E.dma_start(
                    out=b, in_=cc_b[k * 128:(k + 1) * 128, :]), lw, 16)
            else:
                v_l = p.doi("gpsimd", lambda E, b=buf16, k=k: E.indirect_dma_start(
                    out=b, out_offset=None, in_=gx_b.ap(),
                    in_offset=IndirectOffsetOnAxis(oidx_s[:, k:k + 1], 0)), lw, 16)
            p.w("scalar", lw, v_l)
            p.w("scalar", st, st_base + (k - 1) * 16)
            v_ups[k] = p.doi("scalar", lambda E, f=fbuf, b=buf16: E.copy(f, b), sc)
            p.w("sync", sc, v_ups[k])
            p.doi("sync", lambda E, k=k, f=fbuf: E.dma_start(out=dbg[k * 128:(k + 1) * 128, :], in_=f), st, 16)
    p.w("sync", st, p.cur(st))

    if esP_open:
        esPS.close()
        esP.close()

    with p.es, nc.Block() as block:
        p.finalize(block)
    return nc


def _alloc(es, nc, name, shape, dt):
    return es.enter_context(nc.sbuf_tensor(name, shape, dt))



def _consts():
    ident = np.eye(128, dtype=np.float32).astype(BF)
    tau = np.arange(128)[:, None]
    t = np.arange(128)[None, :]
    mask16 = ((tau // 16 == t // 16) & (tau <= t)).astype(np.float32).astype(BF)
    mask16x = ((t // 16 == tau // 16 + 1) & (tau // 32 == t // 32)).astype(np.float32).astype(BF)
    l1m = np.zeros((128, 4, 512), np.float32)
    for o in range(4):
        l1m[:, o, :] = (np.arange(128)[:, None] + o * 128 <= np.arange(512)[None, :])
    l1m = l1m.reshape(128, 4 * 512).astype(BF)
    R = np.zeros((128, 128), np.float32)
    for i in range(32):
        R[2 * i, 2 * i + 1] = -1.0
        R[2 * i + 1, 2 * i] = 1.0
    rot_t = R.T.copy().astype(BF)
    inv_freq = 1.0 / 10000.0 ** (np.arange(0, 64, 2, dtype=np.float32) / 64)
    ang = np.arange(SEQ, dtype=np.float32)[:, None] * inv_freq[None, :]
    cs = np.repeat(np.cos(ang), 2, axis=1).T
    sn = np.repeat(np.sin(ang), 2, axis=1).T
    cos_d = np.concatenate([cs, np.ones((64, SEQ), np.float32)], axis=0)
    sin_d = np.concatenate([sn, np.zeros((64, SEQ), np.float32)], axis=0)
    return dict(ident_b=np.ascontiguousarray(ident), mask16=np.ascontiguousarray(mask16),
                mask16x=np.ascontiguousarray(mask16x),
                l1mask=np.ascontiguousarray(l1m), rot_t=np.ascontiguousarray(rot_t),
                cos_d=np.ascontiguousarray(cos_d), sin_d=np.ascontiguousarray(sin_d))


def make_in_maps(inputs):
    d = inputs
    x = np.asarray(d["L0_attn_hidden_states"])
    consts = _consts()
    shared = dict(
        gw_t=np.ascontiguousarray(np.asarray(d["L0_ffn_gate_w"]).T),
        uw_t=np.ascontiguousarray(np.asarray(d["L0_ffn_up_w"]).T),
        dw_t=np.ascontiguousarray(np.asarray(d["L0_ffn_down_w"]).T),
        gs_r=np.ascontiguousarray(np.asarray(d["L0_ffn_gate_s"]).reshape(NI, 128).T),
        us_r=np.ascontiguousarray(np.asarray(d["L0_ffn_up_s"]).reshape(NI, 128).T),
        ds_r=np.ascontiguousarray(np.asarray(d["L0_ffn_down_s"]).reshape(NK, 128).T),
        wout_t=np.ascontiguousarray(np.asarray(d["L1_attn_w_out"]).T),
        qg_t=np.ascontiguousarray(np.asarray(d["L1_ffn_wq_gate"]).T),
        qu_t=np.ascontiguousarray(np.asarray(d["L1_ffn_wq_up"]).T),
        qd_t=np.ascontiguousarray(np.asarray(d["L1_ffn_wq_down"]).T),
        sg1_r=np.ascontiguousarray(np.asarray(d["L1_ffn_s_gate"]).reshape(NI, 128).T),
        su1_r=np.ascontiguousarray(np.asarray(d["L1_ffn_s_up"]).reshape(NI, 128).T),
        sd1_r=np.ascontiguousarray(np.asarray(d["L1_ffn_s_down"]).reshape(NK, 128).T),
        **consts,
    )
    wq_t = np.ascontiguousarray(np.asarray(d["L0_attn_Wq"]).T)
    wk_t = np.ascontiguousarray(np.asarray(d["L0_attn_Wk"]).T)
    wv_t = np.ascontiguousarray(np.asarray(d["L0_attn_Wv"]).T)
    bq = np.asarray(d["L0_attn_bq"])
    bk = np.asarray(d["L0_attn_bk"])
    bv = np.asarray(d["L0_attn_bv"])
    wqkv_t = np.ascontiguousarray(np.asarray(d["L1_attn_w_qkv"]).T)
    in_maps = []
    for c in range(NCORES):
        b, g = c // 4, c % 4
        m = dict(shared)
        m["x_sl"] = np.ascontiguousarray(x[b, g * TPC:(g + 1) * TPC, :].T)
        m["wq_sl"] = np.ascontiguousarray(wq_t[:, 512 * g:512 * (g + 1)])
        m["wk_sl"] = np.ascontiguousarray(wk_t[:, 128 * g:128 * (g + 1)])
        m["wv_sl"] = np.ascontiguousarray(wv_t[:, 128 * g:128 * (g + 1)])
        m["bq_sl"] = np.ascontiguousarray(bq[512 * g:512 * (g + 1)].reshape(1, 512))
        m["bk_sl"] = np.ascontiguousarray(bk[128 * g:128 * (g + 1)].reshape(1, 128))
        m["bv_sl"] = np.ascontiguousarray(bv[128 * g:128 * (g + 1)].reshape(1, 128))
        m["wqkv_sl"] = np.ascontiguousarray(wqkv_t[:, 1536 * g:1536 * (g + 1)])
        kk = np.arange(NK)
        pp = np.arange(128)
        m["oidx"] = np.ascontiguousarray(
            ((kk[None, :] // 4) * 2048 + g * 512 + (kk[None, :] % 4) * 128 + pp[:, None]).astype(np.int32))
        bm = np.zeros((128, 4), np.float32)
        for i in range(4):
            bm[i * 32:(i + 1) * 32, i] = 1.0
        m["bandm"] = bm
        in_maps.append(m)
    return in_maps


def kernel(**inputs):
    in_maps = make_in_maps(inputs)
    nc = build()
    res = run_bass_kernel_spmd(nc, in_maps, core_ids=list(range(NCORES)))
    out = np.empty((2, SEQ, DM), np.float32)
    for c in range(NCORES):
        b, g = c // 4, c % 4
        out[b, g * TPC:(g + 1) * TPC, :] = res.results[c]["o_sl"].T
    return out
